# version 17
# baseline (speedup 1.0000x reference)
import contextlib
import numpy as np
import ml_dtypes
import concourse.bass as bass
import concourse.mybir as mybir
from concourse.bass_utils import run_bass_kernel_spmd

F32 = mybir.dt.float32
BF16 = mybir.dt.bfloat16
AF = mybir.ActivationFunctionType
ALU = mybir.AluOpType
AX = mybir.AxisListType

NCORE = 8
D = 1024
TCORE = 2048
NT = 16
NT2 = 8
NE = 8
HID = 512
NQ = 8
QSC = 1.0 / 32.0

ENGS = ("pe", "act", "dve", "sp", "pool")
DMA_SEMS = {"sx0", "sx1", "sx2", "sx3", "swc", "swa", "swb", "smsk",
             "skst0", "skst1", "skv0", "skv1", "skv2", "skv3", "skv4",
             "skv5", "svv0", "svv1", "svv2", "svv3", "svv4", "svv5", "sout", "scc0", "scc1", "scc2", "scc3",
             "skc0", "skc1", "skc2", "skc3", "kst0", "kst1", "kst2", "kst3",
             "vst0", "vst1", "vst2", "vst3"}


def _view(t_handle, any_ap, offset, pattern):
    return bass.AP(any_ap.tensor, offset, [[any_ap.ap[0][0], 128]] + pattern)


class KB:
    def __init__(self, nc):
        self.nc = nc
        self.stream = {e: [] for e in ENGS}
        self.cnt = {}
        self.chk = {}
        self.sems = {}
        self.last_wait = {e: {} for e in ENGS}
        self.es = contextlib.ExitStack()

    def sem(self, name):
        self.sems[name] = self.es.enter_context(self.nc.semaphore(name))
        self.cnt[name] = 0

    def do(self, eng, fn, waits=(), inc=None, chk=None):
        wl = []
        for w in waits:
            if w is None:
                continue
            if isinstance(w, str):
                w = self.chk[w]
            s, v, prod = w
            if self.last_wait[eng].get(s, -1) >= v:
                continue
            self.last_wait[eng][s] = v
            wl.append((s, v))
        if inc is not None:
            s, k = inc
            self.cnt[s] += k
            if chk is not None:
                prod = "dma" if s in DMA_SEMS else eng
                self.chk[chk] = (s, self.cnt[s], prod)
        sems = self.sems

        def closure(h, fn=fn, wl=wl, inc=inc):
            for s, v in wl:
                h.wait_ge(sems[s], v)
            r = fn(h)
            if inc is not None:
                r.then_inc(sems[inc[0]], inc[1])
        self.stream[eng].append(closure)

    def waitonly(self, eng, waits):
        self.do(eng, lambda h: None, waits=waits)


def build_program():
    nc = bass.Bass("TRN2", target_bir_lowering=False)
    kb = KB(nc)
    E = kb.es.enter_context

    d_x = nc.dram_tensor("x", [TCORE, D], F32, kind="ExternalInput").ap()
    d_wqT = nc.dram_tensor("wqT", [D, D], F32, kind="ExternalInput").ap()
    d_wkT = nc.dram_tensor("wkT", [D, D], F32, kind="ExternalInput").ap()
    d_wvT = nc.dram_tensor("wvT", [D, D], F32, kind="ExternalInput").ap()
    d_bq = nc.dram_tensor("bq", [8, 128], F32, kind="ExternalInput").ap()
    d_bk = nc.dram_tensor("bk", [8, 128], F32, kind="ExternalInput").ap()
    d_bv = nc.dram_tensor("bv", [1, D], F32, kind="ExternalInput").ap()
    d_gw = nc.dram_tensor("gw", [D, NE], F32, kind="ExternalInput").ap()
    d_w1 = nc.dram_tensor("w1", [NE, D, HID], F32, kind="ExternalInput").ap()
    d_b1 = nc.dram_tensor("b1", [NE, HID], F32, kind="ExternalInput").ap()
    d_w2 = nc.dram_tensor("w2", [NE, HID, D], F32, kind="ExternalInput").ap()
    d_b2 = nc.dram_tensor("b2", [NE, D], F32, kind="ExternalInput").ap()
    d_msk = nc.dram_tensor("msk", [8, 128, 256], F32, kind="ExternalInput").ap()
    d_idf = nc.dram_tensor("idf", [128, 128], F32, kind="ExternalInput").ap()
    d_eb = nc.dram_tensor("eb", [8, 8 * 128], F32, kind="ExternalInput").ap()
    d_out = nc.dram_tensor("out", [TCORE, D], F32, kind="ExternalOutput").ap()

    t_kvself = nc.dram_tensor("kvself", [2 * TCORE, D], BF16, kind="Internal")
    t_kvgath = nc.dram_tensor("kvgath", [8 * TCORE, D], BF16, kind="Internal")
    d_kvself = t_kvself.ap()
    d_kvgath = t_kvgath.ap()

    for s in ("sx0", "sx1", "sx2", "sx3", "swc", "swa", "swb", "smsk",
              "skst0", "skst1",
              "skv0", "skv1", "skv2", "skv3", "skv4", "skv5",
              "svv0", "svv1", "svv2", "svv3", "svv4", "svv5",
              "scc0", "scc1", "scc2", "scc3", "sout",
              "skc0", "skc1", "skc2", "skc3", "kst0", "kst1", "kst2", "kst3",
              "vst0", "vst1", "vst2", "vst3",
              "spe", "sdve", "sact", "spool"):
        kb.sem(s)

    X = E(nc.sbuf_tensor("X", [128, NT, D], F32))
    XHF = E(nc.sbuf_tensor("XHF", [128, 8, TCORE], BF16))
    XHT = E(nc.sbuf_tensor("XHT", [128, 2, D], BF16))
    WA = E(nc.sbuf_tensor("WA", [128, 16384], BF16))
    IDF = E(nc.sbuf_tensor("IDF", [128, 128], F32))
    IDB = E(nc.sbuf_tensor("IDB", [128, 128], BF16))
    GWS = E(nc.sbuf_tensor("GWS", [128, 8, 8], BF16))
    B1T = E(nc.sbuf_tensor("B1T", [128, 4, 8], F32))
    B2B = E(nc.sbuf_tensor("B2B", [8, D], BF16))
    BQT = E(nc.sbuf_tensor("BQT", [128, 8], F32))
    BKT = E(nc.sbuf_tensor("BKT", [128, 8], F32))
    ONESR = E(nc.sbuf_tensor("ONESR", [1, 128], BF16))
    EB = E(nc.sbuf_tensor("EB", [8, 8, 128], BF16))
    ONESC = E(nc.sbuf_tensor("ONESC", [128, 1], BF16))
    CONSTS = E(nc.sbuf_tensor("CONSTS", [128, 8], F32))
    EPSC = CONSTS[:, 0:1]
    CNID = CONSTS[:, 1:2]
    CONE = CONSTS[:, 2:3]
    CHALF = CONSTS[:, 3:4]
    CBIG = CONSTS[:, 4:5]
    CQS = CONSTS[:, 5:6]
    CZERO = CONSTS[:, 6:7]
    CPD = CONSTS[:, 7:8]
    STAT = E(nc.sbuf_tensor("STAT", [128, 7, NT], F32))
    GA = E(nc.sbuf_tensor("GA", [128, NT, 8], F32))
    MTM = E(nc.sbuf_tensor("MTM", [128, NT, 8], F32))
    GST = E(nc.sbuf_tensor("GST", [128, 6, NT], F32))
    GEXP3 = E(nc.sbuf_tensor("GEXP3", [128, NT, 8], F32))
    LINV = E(nc.sbuf_tensor("LINV", [128, 2], F32))
    PTS = E(nc.sbuf_tensor("PTS", [128, 2, 256], BF16))
    JUNK = E(nc.sbuf_tensor("JUNK", [128, D], BF16))

    wa_any = WA[:, 0:1]

    def wa_w1(j):
        return _view(WA, wa_any, j * 4096, [[512, 8], [1, 512]])

    def wa_w2(j):
        return _view(WA, wa_any, 8192 + j * 4096, [[1024, 4], [1, 1024]])

    wa_wk = _view(WA, wa_any, 0, [[1024, 8], [1, 1024]])
    wa_wv = _view(WA, wa_any, 8192, [[1024, 8], [1, 1024]])
    wa_wq = _view(WA, wa_any, 0, [[1024, 8], [1, 1024]])
    BVR = E(nc.sbuf_tensor("BVR", [1, D], BF16))
    KVSTG = E(nc.sbuf_tensor("KVSTG", [128, 4096], BF16))

    def g_dma(dst, src, chkname=None, waits=(), sem="swc"):
        kb.do("pool", lambda h, d=dst, s=src: h.dma_start(out=d, in_=s),
              waits=waits, inc=(sem, 16), chk=chkname)

    kb.do("pool", lambda h: h.memset(ONESR[:, :], 1.0), inc=("spool", 1))
    kb.do("pool", lambda h: h.memset(ONESC[:, :], 1.0), inc=("spool", 1))
    for col, val in ((0, 1e-5), (1, -1.0 / D), (2, 1.0), (3, 0.5),
                     (4, -1e30), (5, QSC), (6, 0.0), (7, 1.0 / D)):
        kb.do("pool", lambda h, c=col, v=val: h.memset(CONSTS[:, c:c + 1], v),
              inc=("spool", 1))
    kb.chk["memsets"] = ("spool", kb.cnt["spool"], "pool")

    g_dma(IDF[:, :], d_idf)
    g_dma(IDB[:, :], d_idf)
    g_dma(GWS[:, :, :], d_gw.rearrange("(ic p) e -> p ic e", p=128))
    g_dma(B2B[:, :], d_b2)
    g_dma(EB[:, :, :], d_eb.rearrange("k (e m) -> k e m", e=8))

    with nc.sbuf_tensor("B1SB", [8, HID], F32) as B1SB, \
         nc.sbuf_tensor("BQSB", [8, 128], F32) as BQSB, \
         nc.sbuf_tensor("BKSB", [8, 128], F32) as BKSB:
        g_dma(B1SB[:, :], d_b1)
        g_dma(BQSB[:, :], d_bq)
        g_dma(BKSB[:, :], d_bk)
        b1sb_ap = B1SB
        bqsb_ap = BQSB
        bksb_ap = BKSB
    g_dma(BVR[0:1, :], d_bv, "consts_all")
    kb.chk["bv_ld"] = kb.chk["consts_all"]

    for g in range(4):
        kb.do("sp", lambda h, g=g: h.dma_start(
            out=X[:, 4 * g:4 * g + 4, :],
            in_=d_x.rearrange("(t p) d -> p t d", p=128)[:, 4 * g:4 * g + 4, :]),
            inc=(f"sx{g}", 16), chk=f"xg{g}")

    def emit_wpair_load(fidx, p, slotview_w1, slotview_w2, waits):
        sem = "swa" if p % 2 == 0 else "swb"
        for j in range(2):
            e = 2 * p + j
            g_dma(slotview_w1(j),
                  d_w1[e].rearrange("(ic p) h -> p ic h", p=128),
                  None, waits=waits, sem=sem)
            g_dma(slotview_w2(j),
                  d_w2[e].rearrange("(hc p) d -> p hc d", p=128),
                  (f"wload_{fidx}_{p}" if j == 1 else None), waits=waits, sem=sem)

    emit_wpair_load(0, 0, wa_w1, wa_w2, ())

    def emit_bvec_transposes(PBF):
        for hc in range(4):
            kb.do("pe", lambda h, hc=hc: h.transpose(
                PBF[:, hc * 8:hc * 8 + 8], b1sb_ap[0:8, hc * 128:(hc + 1) * 128],
                IDF[0:8, 0:8]),
                waits=["consts_all", "consts_all"],
                inc=("spe", 1), chk=f"b1tr{hc}")
        kb.do("pe", lambda h: h.transpose(
            PBF[:, 32:40], bqsb_ap[0:8, :], IDF[0:8, 0:8]),
            waits=["consts_all"], inc=("spe", 1), chk="bqtr")
        kb.do("pe", lambda h: h.transpose(
            PBF[:, 40:48], bksb_ap[0:8, :], IDF[0:8, 0:8]),
            waits=["consts_all"], inc=("spe", 1), chk="bktr")
        kb.do("dve", lambda h: h.tensor_copy(
            B1T[:, :, :], PBF[:, 0:32].rearrange("p (hc e) -> p hc e", hc=4)),
            waits=["b1tr3"], inc=("sdve", 1))
        kb.do("dve", lambda h: h.tensor_copy(BQT[:, :], PBF[:, 32:40]),
              waits=["bqtr"], inc=("sdve", 1))
        kb.do("dve", lambda h: h.tensor_copy(BKT[:, :], PBF[:, 40:48]),
              waits=["bktr"], inc=("sdve", 1), chk="bvecs")

    junk_chain = [None]

    def emit_stats_tile(stage, tau, waits=()):
        sums = STAT[:, 0, :]
        sumsq = STAT[:, 2, :]
        kb.do("dve", lambda h, t=tau: h.reduce_sum(
            sums[:, t:t + 1], X[:, t, :], AX.X),
            waits=list(waits) + ["memsets"],
            inc=("sdve", 1), chk=f"sum{stage}_{tau}")
        kb.do("dve", lambda h, t=tau: h.scalar_tensor_tensor(
            JUNK[:, :], X[:, t, :], CZERO, X[:, t, :],
            ALU.add, ALU.mult, accum_out=sumsq[:, t:t + 1]),
            waits=[junk_chain[0]],
            inc=("sdve", 1), chk=f"var{stage}_{tau}")
        junk_chain[0] = f"var{stage}_{tau}"

    def emit_prep(stage, gating, MEM, xwait, entry_pe, entry_dve,
                  stats_done=False):
        sums = STAT[:, 0, :]
        nmean = STAT[:, 1, :]
        sumsq = STAT[:, 2, :]
        m2 = STAT[:, 3, :]
        var_ = STAT[:, 4, :]
        std = STAT[:, 5, :]
        rstd = STAT[:, 6, :]

        with nc.psum_tensor("PT0_%d" % stage, [128, 512], BF16) as PT0, \
             nc.psum_tensor("PT1_%d" % stage, [128, 512], BF16) as PT1, \
             nc.psum_tensor("PG0_%d" % stage, [128, 8], F32) as PG0, \
             nc.psum_tensor("PG1_%d" % stage, [128, 8], F32) as PG1, \
             nc.psum_tensor("PBF_%d" % stage, [128, 512], F32) as PBF:
            PT = (PT0, PT1)
            PG = (PG0, PG1)
            if stage == 0:
                emit_bvec_transposes(PBF)

            if not stats_done:
                for tau in range(NT):
                    emit_stats_tile(stage, tau, waits=[xwait(tau)])
            kb.do("dve", lambda h: h.tensor_scalar(
                nmean[:, :], sums[:, :], CNID, None, ALU.mult),
                waits=[f"sum{stage}_{NT-1}", f"var{stage}_{NT-1}"],
                inc=("sdve", 1), chk=f"nmean{stage}")
            kb.do("dve", lambda h: h.tensor_mul(m2[:, :], nmean[:, :], nmean[:, :]),
                  waits=[f"nmean{stage}"], inc=("sdve", 1), chk=f"m2{stage}")
            kb.do("dve", lambda h: h.scalar_tensor_tensor(
                var_[:, :], sumsq[:, :], CPD, m2[:, :], ALU.mult, ALU.subtract),
                waits=[f"m2{stage}"], inc=("sdve", 1), chk=f"varb{stage}")
            kb.do("act", lambda h: h.activation(
                std[:, :], var_[:, :], AF.Sqrt, bias=EPSC, scale=1.0),
                waits=[f"varb{stage}", "memsets"],
                inc=("sact", 1), chk=f"std{stage}")
            kb.do("dve", lambda h: h.reciprocal(rstd[:, :], std[:, :]),
                  waits=[f"std{stage}"], inc=("sdve", 1), chk=f"rstd{stage}")

            for tau in range(NT):
                par = tau % 2
                kb.do("dve", lambda h, t=tau, par=par: h.tensor_scalar(
                    XHT[:, par, :], X[:, t, :], nmean[:, t:t + 1],
                    rstd[:, t:t + 1], ALU.add, ALU.mult),
                    waits=[f"rstd{stage}",
                           f"tr{stage}_{tau-2}_1" if tau >= 2 else None],
                    inc=("sdve", 1), chk=f"app{stage}_{tau}")
                for grp in range(2):
                    for k in range(4):
                        dc = grp * 4 + k
                        kb.do("pe", lambda h, t=tau, par=par, dc=dc, grp=grp: h.transpose(
                            PT[(2 * t + grp) % 2][:, (dc % 4) * 128:(dc % 4 + 1) * 128],
                            XHT[:, par, dc * 128:(dc + 1) * 128], IDB[:, :]),
                            waits=[f"app{stage}_{tau}", "consts_all",
                                   (f"trev{stage}_{tau-1}_{grp}" if tau >= 1 else entry_dve),
                                   (entry_pe if tau == 0 and dc == 0 else None)],
                            inc=("spe", 1),
                            chk=(f"tr{stage}_{tau}_{grp}" if k == 3 else None))
                    kb.do("dve", lambda h, t=tau, grp=grp: h.tensor_copy(
                        XHF[:, 4 * grp:4 * grp + 4, t * 128:(t + 1) * 128],
                        PT[(2 * t + grp) % 2][:, 0:512].rearrange(
                            "p (a b) -> p a b", a=4)),
                        waits=[f"tr{stage}_{tau}_{grp}"],
                        inc=("sdve", 1), chk=f"trev{stage}_{tau}_{grp}")
                if gating:
                    for ic in range(8):
                        kb.do("pe", lambda h, t=tau, ic=ic, par=par: h.matmul(
                            PG[par][:, :], XHF[:, ic, t * 128:(t + 1) * 128],
                            GWS[:, ic, :], start=(ic == 0), stop=(ic == 7)),
                            waits=[f"trev{stage}_{tau}_1", "consts_all",
                                   f"gev{stage}_{tau-2}" if tau >= 2 else None],
                            inc=(("spe", 1) if ic == 7 else None),
                            chk=(f"g{stage}_{tau}" if ic == 7 else None))
                    kb.do("dve", lambda h, t=tau, par=par: h.tensor_copy(
                        GA[:, t, :], PG[par][:, :]),
                        waits=[f"g{stage}_{tau}"],
                        inc=("sdve", 1), chk=f"gev{stage}_{tau}")

            kb.chk[f"prepdone{stage}"] = kb.chk[f"trev{stage}_{NT-1}_1"]

            if gating:
                g_s1 = GST[:, 0, :]
                g_z = GST[:, 1, :]
                g_s2 = GST[:, 2, :]
                g_d = GST[:, 3, :]
                g_e2 = GST[:, 4, :]
                g_msc = GST[:, 5, :]

                def bc3(ap2d):
                    return bass.AP(ap2d.tensor, ap2d.offset,
                                   [[ap2d.ap[0][0], 128], [1, NT], [0, 8]])

                nstp = [0]

                def gb(fn, waits=(), engine="dve"):
                    nstp[0] += 1
                    name = f"gb{stage}_{nstp[0]}"
                    prev = f"gb{stage}_{nstp[0]-1}" if nstp[0] > 1 else None
                    kb.do(engine, fn, waits=list(waits) + [prev],
                          inc=(("sdve", 1) if engine == "dve" else ("sact", 1)),
                          chk=name)
                    return name

                gb(lambda h: h.reduce_max(g_s1, GA[:, :, :], AX.X),
                   waits=[f"gev{stage}_{NT-1}"])
                gb(lambda h: h.tensor_tensor(GEXP3[:, :, :], GA[:, :, :],
                                             bc3(g_s1), ALU.subtract))
                gb(lambda h: h.activation(MTM[:, :, :], GEXP3[:, :, :], AF.Exp),
                   engine="act")
                gb(lambda h: h.reduce_sum(g_z, MTM[:, :, :], AX.X))
                gb(lambda h: h.tensor_tensor(GEXP3[:, :, :], GA[:, :, :],
                                             bc3(g_s1), ALU.is_ge))
                gb(lambda h: h.scalar_tensor_tensor(MTM[:, :, :], GEXP3[:, :, :],
                                                    CBIG, GA[:, :, :],
                                                    ALU.mult, ALU.add))
                gb(lambda h: h.reduce_max(g_s2, MTM[:, :, :], AX.X))
                gb(lambda h: h.tensor_sub(g_d, g_s2, g_s1))
                gb(lambda h: h.activation(g_e2, g_d, AF.Exp), engine="act")
                gb(lambda h: h.reciprocal(g_d, g_z))
                gb(lambda h: h.tensor_scalar(g_e2, g_e2, CONE, None, ALU.add))
                gb(lambda h: h.tensor_mul(g_msc, g_e2, g_d))
                gb(lambda h: h.tensor_scalar(g_msc, g_msc, CHALF, None, ALU.mult))
                gb(lambda h: h.tensor_tensor(GEXP3[:, :, :], GA[:, :, :],
                                             bc3(g_s2), ALU.is_ge))
                last = gb(lambda h: h.tensor_tensor(MTM[:, :, :], GEXP3[:, :, :],
                                                    bc3(g_msc), ALU.mult))
                for tau in range(NT):
                    kb.chk[f"mtm_{stage}_{tau}"] = kb.chk[last]

                for g4 in range(4):
                    for k in range(4):
                        tau = g4 * 4 + k
                        kb.do("pe", lambda h, tau=tau, k=k: h.transpose(
                            PBF[0:8, k * 128:(k + 1) * 128], MTM[:, tau, :],
                            IDF[:, :]),
                            waits=[f"mtm_{stage}_{tau}",
                                   f"memev_{stage}_{g4-1}" if g4 >= 1 else ("bvecs" if stage == 0 else None)],
                            inc=("spe", 1),
                            chk=(f"mtr_{stage}_{g4}" if k == 3 else None))
                    kb.do("dve", lambda h, g4=g4: h.tensor_copy(
                        MEM[0:8, g4 * 512:(g4 + 1) * 512], PBF[0:8, 0:512]),
                        waits=[f"mtr_{stage}_{g4}"],
                        inc=("sdve", 1), chk=f"memev_{stage}_{g4}")
                kb.chk[f"memdone{stage}"] = kb.chk[f"memev_{stage}_3"]

    FFNPSUM = [None]

    def emit_ffn(fidx, stage, MEM, MPP, HS, wb_w1, wb_w2, entry_pe, entry_dve,
                 entry_gelu, on_yev=None, on_pass_end=None):
        with nc.psum_tensor("PH0_%d" % fidx, [128, 1024], F32) as PH0, \
             nc.psum_tensor("PH1_%d" % fidx, [128, 1024], F32) as PH1, \
             nc.psum_tensor("PY0_%d" % fidx, [128, 1024], F32) as PY0, \
             nc.psum_tensor("PY1_%d" % fidx, [128, 1024], F32) as PY1:
            FFNPSUM[0] = (PH0, PH1, PY0, PY1)
            PH = (PH0, PH1)
            PY = (PY0, PY1)
            for p in range(4):
                w1v = wa_w1 if p % 2 == 0 else wb_w1
                w2v = wa_w2 if p % 2 == 0 else wb_w2
                for j in range(2):
                    e = 2 * p + j
                    for qr in range(4):
                        hb = qr % 2
                        kb.do("pe", lambda h, e=e, qr=qr, hb=hb: h.matmul(
                            PH0[:, hb * 512:(hb + 1) * 512], EB[0:8, e, :],
                            MEM[0:8, qr * 512:(qr + 1) * 512],
                            start=True, stop=True),
                            waits=[f"memdone{stage}", "consts_all", kb.chk.get("memsets"),
                                   (f"f{fidx}_gelu_{p-1}_7_0" if p >= 1 else entry_gelu),
                                   (entry_pe if p == 0 and j == 0 and qr == 0 else None),
                                   (f"mbev_{fidx}_{p}_{j}_{qr-2}" if qr >= 2 else
                                    (f"mbev_{fidx}_{p}_{j-1}_{qr+2}" if j >= 1 else
                                     (f"mbev_{fidx}_{p-1}_1_{qr+2}" if p >= 1 else None)))],
                            inc=("spe", 1), chk=f"mb_{fidx}_{p}_{j}_{qr}")
                        kb.do("dve", lambda h, j=j, qr=qr, hb=hb: h.tensor_copy(
                            MPP[:, j, qr * 512:(qr + 1) * 512],
                            PH0[:, hb * 512:(hb + 1) * 512]),
                            waits=[f"mb_{fidx}_{p}_{j}_{qr}",
                                   (f"f{fidx}_hm_{p-1}_7_{j}" if p >= 1 else None)],
                            inc=("sdve", 1), chk=f"mbev_{fidx}_{p}_{j}_{qr}")

                def emit_h(t, j):
                    e = 2 * p + j
                    wchk = f"wload_{fidx}_{p}"
                    for hc in range(4):
                        for ic in range(8):
                            last = (hc == 3 and ic == 7)
                            kb.do("pe", lambda h, t=t, j=j, hc=hc, ic=ic, w1v=w1v:
                                  h.matmul(PH[j][:, hc * 256:(hc + 1) * 256],
                                           w1v(j)[:, ic, hc * 128:(hc + 1) * 128],
                                           XHF[:, ic, t * 256:(t + 1) * 256],
                                           start=(ic == 0),
                                           stop=(ic == 7)),
                                  waits=[wchk, f"prepdone{stage}",
                                         f"mbev_{fidx}_{p}_1_3" if j == 0 and t == 0 else None,
                                         (f"f{fidx}_gelu_{p}_{t-1}_{j}" if t >= 1 else
                                          (f"f{fidx}_gelu_{p-1}_7_{j}" if p >= 1 else None))],
                                  inc=(("spe", 1) if last else None),
                                  chk=(f"f{fidx}_h_{p}_{t}_{j}" if last else None))
                    tp = t % 2
                    if t >= 2:
                        hswar = f"f{fidx}_y_{p}_{t-2}"
                    elif p >= 1:
                        hswar = f"f{fidx}_y_{p-1}_{6 + tp}"
                    else:
                        hswar = entry_pe if fidx == 1 else None
                    for hc in range(4):
                        kb.do("act", lambda h, t=t, j=j, hc=hc, e=e, tp=tp: h.activation(
                            HS[:, j, tp, hc * 256:(hc + 1) * 256],
                            PH[j][:, hc * 256:(hc + 1) * 256],
                            AF.Gelu, bias=B1T[:, hc, e:e + 1]),
                            waits=[f"f{fidx}_h_{p}_{t}_{j}", "bvecs", hswar],
                            inc=("sact", 1),
                            chk=(f"f{fidx}_gelu_{p}_{t}_{j}" if hc == 3 else None))
                    mpslice = MPP[:, j, t * 256:(t + 1) * 256]
                    mp_b = bass.AP(mpslice.tensor, mpslice.offset,
                                   [[mpslice.ap[0][0], 128], [0, 4], [1, 256]])
                    hview = HS[:, j, tp, :].rearrange("p (a b) -> p a b", a=4)
                    kb.do("dve", lambda h, hview=hview, mp_b=mp_b: h.tensor_tensor(
                        hview, hview, mp_b, ALU.mult),
                        waits=[f"f{fidx}_gelu_{p}_{t}_{j}", f"mbev_{fidx}_{p}_{j}_3"],
                        inc=("sdve", 1), chk=f"f{fidx}_hm_{p}_{t}_{j}")

                def emit_y(t):
                    if p == 0:
                        for ts in range(2):
                            for dn in range(2):
                                kb.do("pe", lambda h, t=t, ts=ts, dn=dn: h.matmul(
                                    PY[ts][:, dn * 512:(dn + 1) * 512],
                                    MEM[0:8, (2 * t + ts) * 128:(2 * t + ts + 1) * 128],
                                    B2B[0:8, dn * 512:(dn + 1) * 512],
                                    start=True, stop=False, skip_group_check=True),
                                    waits=[f"memdone{stage}", "consts_all",
                                           (f"f{fidx}_yev_{p}_{t-1}" if t >= 1 else entry_dve)],
                                    inc=None)
                    for j in range(2):
                        wchk = f"wload_{fidx}_{p}"
                        for ts in range(2):
                            for hc in range(4):
                                for dn in range(2):
                                    first = (p > 0 and j == 0 and hc == 0)
                                    last = (j == 1 and ts == 1 and hc == 3 and dn == 1)
                                    kb.do("pe", lambda h, t=t, j=j, ts=ts, hc=hc, dn=dn, w2v=w2v, st=first:
                                          h.matmul(PY[ts][:, dn * 512:(dn + 1) * 512],
                                                   HS[:, j, t % 2, hc * 256 + ts * 128:hc * 256 + (ts + 1) * 128],
                                                   w2v(j)[:, hc, dn * 512:(dn + 1) * 512],
                                                   start=st,
                                                   stop=(j == 1 and hc == 3),
                                                   skip_group_check=True),
                                          waits=[wchk, f"f{fidx}_hm_{p}_{t}_{j}",
                                                 (f"f{fidx}_yev_{p}_{t-1}" if (p > 0 and t >= 1) else None),
                                                 (f"f{fidx}_yev_{p-1}_7" if (p > 0 and t == 0) else None),
                                                 (f"wvev_{(t-2)//2}_7" if (fidx == 0 and p == 3 and t >= 2 and t % 2 == 0) else None)],
                                          inc=(("spe", 1) if last else None),
                                          chk=(f"f{fidx}_y_{p}_{t}" if last else None))
                    for ts in range(2):
                        kb.do("dve", lambda h, t=t, ts=ts: h.tensor_tensor(
                            X[:, 2 * t + ts, :], PY[ts][:, :], X[:, 2 * t + ts, :],
                            ALU.add),
                            waits=[f"f{fidx}_y_{p}_{t}",
                                   (f"f{fidx}_yev_{p-1}_{t}" if p >= 1 else None)],
                            inc=("sdve", 1),
                            chk=(f"f{fidx}_yev_{p}_{t}" if ts == 1 else None))
                    if p == 3 and on_yev is not None:
                        on_yev(t)

                emit_h(0, 0)
                emit_h(0, 1)
                for t in range(1, NT2):
                    emit_h(t, 0)
                    emit_h(t, 1)
                    emit_y(t - 1)
                emit_y(NT2 - 1)
                if on_pass_end is not None:
                    on_pass_end(p)

                if p + 2 <= 3:
                    tgt_w1 = wa_w1 if (p + 2) % 2 == 0 else wb_w1
                    tgt_w2 = wa_w2 if (p + 2) % 2 == 0 else wb_w2
                    emit_wpair_load(fidx, p + 2,
                                    lambda j, v=tgt_w1: v(j),
                                    lambda j, v=tgt_w2: v(j),
                                    waits=[f"f{fidx}_y_{p}_{NT2-1}"])
            kb.chk[f"ffndone{fidx}"] = kb.chk[f"f{fidx}_yev_3_{NT2-1}"]
            kb.chk[f"ffngelu{fidx}"] = kb.chk[f"f{fidx}_gelu_3_{NT2-1}_1"]

    def emit_weave_chunk(tb):
        PH0, PH1, PY0, PY1 = FFNPSUM[0]
        PY = (PY0, PY1)
        _m = KVSTG[0:1, 0:1]
        pstr = _m.ap[0][0]

        def kstv(slot):
            return bass.AP(_m.tensor, slot * 512,
                           [[pstr, 128], [128, 4], [1, 128]])

        def vstv(slot):
            return bass.AP(_m.tensor, 2048 + slot * 512, [[pstr, 128], [1, 512]])

        sums = STAT[:, 0, :]
        nmean = STAT[:, 1, :]
        sumsq = STAT[:, 2, :]
        m2 = STAT[:, 3, :]
        var_ = STAT[:, 4, :]
        std = STAT[:, 5, :]
        rstd = STAT[:, 6, :]
        T4 = slice(4 * tb, 4 * tb + 4)
        kb.do("dve", lambda h: h.tensor_scalar(
            nmean[:, T4], sums[:, T4], CNID, None, ALU.mult),
            waits=[f"var{1}_{4*tb+3}", "memsets"],
            inc=("sdve", 1), chk=f"wnm_{tb}")
        kb.do("dve", lambda h: h.tensor_mul(m2[:, T4], nmean[:, T4], nmean[:, T4]),
              waits=[f"wnm_{tb}"], inc=("sdve", 1), chk=f"wm2_{tb}")
        kb.do("dve", lambda h: h.scalar_tensor_tensor(
            var_[:, T4], sumsq[:, T4], CPD, m2[:, T4], ALU.mult, ALU.subtract),
            waits=[f"wm2_{tb}"], inc=("sdve", 1), chk=f"wvr_{tb}")
        kb.do("act", lambda h: h.activation(
            std[:, T4], var_[:, T4], AF.Sqrt, bias=EPSC, scale=1.0),
            waits=[f"wvr_{tb}"], inc=("sact", 1), chk=f"wsd_{tb}")
        kb.do("dve", lambda h: h.reciprocal(rstd[:, T4], std[:, T4]),
              waits=[f"wsd_{tb}"], inc=("sdve", 1), chk=f"wrs_{tb}")
        for tloc in range(4):
            tau = 4 * tb + tloc
            par = tau % 2
            if tb == 0 and tloc < 2:
                xhtw = f"tr0_{14 + tloc}_1"
            else:
                pt = tau - 2
                xhtw = f"wtr_{pt // 4}_{2 * (pt % 4) + 1}"
            kb.do("dve", lambda h, tau=tau, par=par: h.tensor_scalar(
                XHT[:, par, :], X[:, tau, :], nmean[:, tau:tau + 1],
                rstd[:, tau:tau + 1], ALU.add, ALU.mult),
                waits=[f"wrs_{tb}", xhtw],
                inc=("sdve", 1), chk=f"wap_{tb}_{tloc}")
            for grp in range(2):
                b = 2 * tloc + grp
                for k in range(4):
                    dc = grp * 4 + k
                    kb.do("pe", lambda h, tau=tau, par=par, dc=dc, b=b: h.matmul(
                        PY[b % 2][:, 512 + (dc % 4) * 128:512 + (dc % 4 + 1) * 128],
                        XHT[:, par, dc * 128:(dc + 1) * 128], IDB[:, :],
                        start=True, stop=True),
                        waits=[f"wap_{tb}_{tloc}", f"f0_yev_3_{2*tb+1}",
                               (f"wtrev_{tb}_{b-2}" if b >= 2 else None)],
                        inc=("spe", 1),
                        chk=(f"wtr_{tb}_{b}" if k == 3 else None))
                kb.do("act", lambda h, tau=tau, grp=grp, b=b: h.activation(
                    XHF[:, 4 * grp:4 * grp + 4, tau * 128:(tau + 1) * 128],
                    PY[b % 2][:, 512:1024].rearrange("p (a c) -> p a c", a=4),
                    AF.Copy),
                    waits=[f"wtr_{tb}_{b}"],
                    inc=("sact", 1), chk=f"wtrev_{tb}_{b}")
        for dc in range(8):
            bank = dc % 2
            for ic in range(8):
                kb.do("pe", lambda h, dc=dc, ic=ic, bank=bank, tb=tb: h.matmul(
                    PY[bank][:, 0:512], wa_wk[:, ic, dc * 128:(dc + 1) * 128],
                    XHF[:, ic, tb * 512:(tb + 1) * 512],
                    start=(ic == 0), stop=(ic == 7)),
                    waits=["wkv_ld", f"wtrev_{tb}_7",
                           (f"wkev_{tb}_{dc-2}" if dc >= 2 else None)],
                    inc=(("spe", 1) if ic == 7 else None),
                    chk=(f"wkmm_{tb}_{dc}" if ic == 7 else None))
            kb.do("act", lambda h, dc=dc, bank=bank: h.activation(
                kstv(dc % 4),
                PY[bank][:, 0:512].rearrange("p (b t) -> p b t", b=4),
                AF.Identity, bias=BKT[:, dc:dc + 1]),
                waits=[f"wkmm_{tb}_{dc}", "bvecs",
                       ((f"wkst_{tb}_{dc-4}") if dc >= 4 else
                        ((f"wkst_{tb-1}_{dc+4}") if tb >= 1 else None))],
                inc=("sact", 1), chk=f"wkev_{tb}_{dc}")
            dst = bass.AP(d_kvself.tensor, tb * 1024 * 1024 + 128 * dc,
                          [[1024, 128], [131072, 4], [1, 128]])
            kb.do("sp", lambda h, dst=dst, dc=dc: h.dma_start(
                out=dst, in_=kstv(dc % 4)),
                waits=[f"wkev_{tb}_{dc}"], inc=(f"kst{dc % 4}", 16),
                chk=f"wkst_{tb}_{dc}")
        for k4 in range(8):
            ts16 = 4 * tb + k4 // 2
            dn = k4 % 2
            bank = k4 % 2
            for ic in range(8):
                kb.do("pe", lambda h, ts16=ts16, dn=dn, ic=ic, bank=bank: h.matmul(
                    PY[bank][:, 0:512], XHF[:, ic, ts16 * 128:(ts16 + 1) * 128],
                    wa_wv[:, ic, dn * 512:(dn + 1) * 512],
                    start=(ic == 0), stop=False),
                    waits=[(f"wkev_{tb}_{6+k4}" if k4 < 2 else f"wvev_{tb}_{k4-2}")],
                    inc=None)
            kb.do("pe", lambda h, dn=dn, bank=bank: h.matmul(
                PY[bank][:, 0:512], ONESR[:, :], BVR[0:1, dn * 512:(dn + 1) * 512],
                start=False, stop=True),
                waits=["bv_ld", "memsets"],
                inc=("spe", 1), chk=f"wvmm_{tb}_{k4}")
            kb.do("act", lambda h, k4=k4, bank=bank: h.activation(
                vstv(k4 % 4), PY[bank][:, 0:512], AF.Copy),
                waits=[f"wvmm_{tb}_{k4}",
                       ((f"wvst_{tb}_{k4-4}") if k4 >= 4 else
                        ((f"wvst_{tb-1}_{k4+4}") if tb >= 1 else None))],
                inc=("sact", 1), chk=f"wvev_{tb}_{k4}")
            dst = bass.AP(d_kvself.tensor,
                          (tb * 1024 + 512 + (k4 // 2) * 128) * 1024 + dn * 512,
                          [[1024, 128], [1, 512]])
            kb.do("sp", lambda h, dst=dst, k4=k4: h.dma_start(
                out=dst, in_=vstv(k4 % 4)),
                waits=[f"wvev_{tb}_{k4}"], inc=(f"vst{k4 % 4}", 16),
                chk=f"wvst_{tb}_{k4}")
    def emit_cc(tc, tgate):
        kb.do("pool", lambda h, tc=tc: h.collective_compute(
            "AllGather", ALU.bypass,
            replica_groups=[[0, 1, 2, 3], [4, 5, 6, 7]],
            ins=[d_kvself[tc * 1024:(tc + 1) * 1024, :].bitcast(F32).opt()],
            outs=[d_kvgath[tc * 4096:(tc + 1) * 4096, :].bitcast(F32).opt()]),
            waits=[c for t_ in (tc, tgate) for s_ in range(4, 8)
                   for c in (f"wkst_{t_}_{s_}", f"wvst_{t_}_{s_}")],
            inc=(f"scc{tc}", 1), chk=f"cc{tc}")

    def emit_qkv_q():
        with nc.psum_tensor("PQ0", [128, 512], F32) as PQ0, \
             nc.psum_tensor("PQ1", [128, 512], F32) as PQ1:
            PQ = (PQ0, PQ1)
            pq = 0
            pq_chk = {}
            for tb in range(4):
                for dc in range(8):
                    par = pq % 2
                    for ic in range(8):
                        kb.do("pe", lambda h, dc=dc, tb=tb, ic=ic, par=par: h.matmul(
                            PQ[par][:, :], wa_wq[:, ic, dc * 128:(dc + 1) * 128],
                            XHF[:, ic, tb * 512:(tb + 1) * 512],
                            start=(ic == 0), stop=(ic == 7)),
                            waits=["wq_ld", "ffngelu0", "ffndone0", "wvev_3_7",
                                   pq_chk.get(par)],
                            inc=(("spe", 1) if ic == 7 else None),
                            chk=(f"qmm_{tb}_{dc}" if ic == 7 else None))
                    kb.do("dve", lambda h, dc=dc, tb=tb, par=par: h.tensor_scalar(
                        Q[:, dc, tb * 512:(tb + 1) * 512], PQ[par][:, :],
                        BQT[:, dc:dc + 1], CQS, ALU.add, ALU.mult),
                        waits=[f"qmm_{tb}_{dc}", "bvecs", "memsets"],
                        inc=("sdve", 1),
                        chk=f"qev_{tb}_{dc}")
                    pq_chk[par] = f"qev_{tb}_{dc}"
                    pq += 1
                kb.chk[f"qevtb{tb}"] = kb.chk[f"qev_{tb}_7"]
            kb.chk["qmmall"] = kb.chk["qmm_3_7"]
            kb.chk["qall"] = kb.chk["qev_3_7"]

    def emit_attention(MSKC, KBUF, VBUF, on_evac=None):
        with nc.psum_tensor("PS0", [128, 256], F32) as PS0, \
             nc.psum_tensor("PS1", [128, 256], F32) as PS1, \
             nc.psum_tensor("PO0", [128, 1024], F32) as PO0, \
             nc.psum_tensor("PO1", [128, 1024], F32) as PO1, \
             nc.psum_tensor("PLA", [128, 1], F32) as PLA, \
             nc.psum_tensor("PLB", [128, 1], F32) as PLB:
            PS = (PS0, PS1)
            PO = (PO0, PO1)
            steps = []
            for qb in range(NQ):
                kn = 8 * qb + 8
                for t in range(kn):
                    steps.append((qb, t, kn))
            nsteps = len(steps)

            def kv_load(i):
                qb, t, kn = steps[i]
                g = t
                v_ = (g // 2) % 4
                j2 = g // 8
                u_ = 2 * j2 + (g % 2)
                tbc = j2 // 2
                bp = u_ - 4 * tbc
                krow = 4096 * tbc + 1024 * v_ + 128 * bp
                vrow = krow + 512
                buf = i % 4
                w = [f"cc{tbc}",
                     (f"AV_{i-4}" if i >= 4 else "f0_y_3_7")]
                kb.do("sp", lambda h, krow=krow, buf=buf: h.dma_start(
                    out=KBUF[:, buf, :],
                    in_=d_kvgath[krow:krow + 128, :]),
                    waits=w, inc=(f"skv{buf}", 16), chk=f"kvldK_{i}")
                kb.do("sp", lambda h, vrow=vrow, buf=buf: h.dma_start(
                    out=VBUF[:, buf, :],
                    in_=d_kvgath[vrow:vrow + 128, :]),
                    waits=[], inc=(f"skv{buf}", 16), chk=f"kvld_{i}")

            def emit_S(i):
                qb, t, kn = steps[i]
                buf = i % 4
                kbv = KBUF[:, buf, :].rearrange("p (c t) -> p c t", c=8)
                for dc in range(8):
                    kb.do("pe", lambda h, dc=dc, kbv=kbv, qb=qb, i=i: h.matmul(
                        PS[i % 2][:, :], kbv[:, dc, :],
                        Q[:, dc, qb * 256:(qb + 1) * 256],
                        start=(dc == 0), stop=(dc == 7)),
                        waits=[f"kvld_{i}", f"kvldK_{i}", f"qevtb{qb // 2}",
                               (f"P_{i-2}" if i >= 2 else None),
                               ("qall" if i == 0 else None)],
                        inc=(("spe", 1) if dc == 7 else None),
                        chk=(f"S_{i}" if dc == 7 else None))
                j = t - 8 * qb
                if j >= 0:
                    kb.do("dve", lambda h, i=i, j=j: h.tensor_tensor(
                        PS[i % 2][:, :], PS[i % 2][:, :], MSKC[:, j, :], ALU.add),
                        waits=[f"S_{i}", "msk_ld"],
                        inc=("sdve", 1), chk=f"MS_{i}")
                kb.do("act", lambda h, i=i: h.activation(
                    PTS[:, i % 2, :], PS[i % 2][:, :], AF.Exp),
                    waits=[f"S_{i}", (f"MS_{i}" if j >= 0 else None),
                           (f"AV_{i-2}" if i >= 2 else None)],
                    inc=("sact", 1), chk=f"P_{i}")

            def emit_AV(i):
                qb, t, kn = steps[i]
                buf = i % 4
                first = (t == 0)
                lastt = (t == kn - 1)
                for qs in range(2):
                    for dn in range(2):
                        kb.do("pe", lambda h, i=i, qs=qs, dn=dn, buf=buf: h.matmul(
                            PO[qs][:, dn * 512:(dn + 1) * 512],
                            PTS[:, i % 2, qs * 128:(qs + 1) * 128],
                            VBUF[:, buf, dn * 512:(dn + 1) * 512],
                            start=first, stop=lastt, skip_group_check=True),
                            waits=[f"P_{i}",
                                   ((f"attev_{qb-1}" if qb >= 1 else "qall")
                                    if first and qs == 0 and dn == 0 else None)],
                            inc=None)
                    kb.do("pe", lambda h, i=i, qs=qs: h.matmul(
                        (PLA if qs == 0 else PLB)[:, 0:1],
                        PTS[:, i % 2, qs * 128:(qs + 1) * 128],
                        ONESC[:, :],
                        start=first, stop=lastt,
                        skip_group_check=True),
                        waits=[],
                        inc=(("spe", 1) if qs == 1 else None),
                        chk=(f"AV_{i}" if qs == 1 else None))

            def emit_evac(qb, i):
                kb.do("dve", lambda h: h.reciprocal(LINV[:, 0:1], PLA[:, 0:1]),
                      waits=[f"AV_{i}"], inc=("sdve", 1))
                kb.do("dve", lambda h: h.reciprocal(LINV[:, 1:2], PLB[:, 0:1]),
                      inc=("sdve", 1), chk=f"linv_{qb}")
                for qs in range(2):
                    kb.do("dve", lambda h, qb=qb, qs=qs: h.scalar_tensor_tensor(
                        X[:, 2 * qb + qs, :], PO[qs][:, :], LINV[:, qs:qs + 1],
                        X[:, 2 * qb + qs, :], ALU.mult, ALU.add),
                        waits=[f"linv_{qb}"],
                        inc=("sdve", 1),
                        chk=(f"attev_{qb}" if qs == 1 else None))

            for i0 in range(4):
                kv_load(i0)
            emit_S(0)
            for i in range(nsteps):
                if i + 1 < nsteps:
                    emit_S(i + 1)
                emit_AV(i)
                if i + 4 < nsteps:
                    kv_load(i + 4)
                qb, t, kn = steps[i]
                if t == kn - 1:
                    emit_evac(qb, i)
                    if on_evac is not None:
                        on_evac(qb)

    Q_box = [None]

    MEM0_BOX = [None]
    with nc.sbuf_tensor("MEM0", [128, TCORE], BF16) as MEM0, \
         nc.sbuf_tensor("MPP0", [128, 2, TCORE], BF16) as MPP0, \
         nc.sbuf_tensor("HS0", [128, 2, 2, 1024], BF16) as HS0, \
         nc.sbuf_tensor("WB0", [128, 16384], BF16) as WB0:
        wb_any = WB0[:, 0:1]
        wb_w1 = lambda j: _view(WB0, wb_any, j * 4096, [[512, 8], [1, 512]])
        wb_w2 = lambda j: _view(WB0, wb_any, 8192 + j * 4096, [[1024, 4], [1, 1024]])
        emit_wpair_load(0, 1, wb_w1, wb_w2, ())
        emit_prep(0, True, MEM0,
                  lambda tau: f"xg{tau // 4}",
                  None, None)
        MEM0_BOX[0] = MEM0

        def _ln2_stats(t):
            emit_stats_tile(1, 2 * t, waits=[f"f0_yev_3_{t}"])
            emit_stats_tile(1, 2 * t + 1)
            if t % 2 == 1:
                emit_weave_chunk(t // 2)

        def _wkv_loads(p):
            if p == 2:
                g_dma(wa_wk, d_wkT.rearrange("(ic p) o -> p ic o", p=128),
                      None, waits=["f0_y_2_7"], sem="swa")
                g_dma(wa_wv, d_wvT.rearrange("(ic p) o -> p ic o", p=128),
                      "wkv_ld", waits=[], sem="swa")
        emit_ffn(0, 0, MEM0, MPP0, HS0, wb_w1, wb_w2, None, None, None,
                 on_yev=_ln2_stats, on_pass_end=_wkv_loads)

    q_scope = contextlib.ExitStack()
    Q = q_scope.enter_context(nc.sbuf_tensor("Q", [128, 8, TCORE], BF16))
    MSKC = q_scope.enter_context(nc.sbuf_tensor("MSKC", [128, 8, 256], BF16))
    Q_box[0] = Q
    if True:
        for tc in range(4):
            emit_cc(tc, 3)
        mskf = PTS[:, :, :].rearrange("p a b -> p (a b)").bitcast(F32)
        msk_src = d_msk.rearrange("j p q -> p j q")
        for j in range(8):
            kb.do("sp", lambda h, j=j: h.dma_start(
                out=mskf, in_=msk_src[:, j, :]),
                waits=[("f0_y_3_7" if j == 0 else f"mkc_{j-1}")],
                inc=("smsk", 16), chk=f"mkl_{j}")
            kb.do("act", lambda h, j=j: h.activation(
                MSKC[:, j, :], mskf, AF.Copy),
                waits=[f"mkl_{j}"],
                inc=("sact", 1), chk=f"mkc_{j}")
        kb.chk["msk_ld"] = kb.chk["mkc_7"]
        xwf = XHT[:, :, :].rearrange("p a b -> p (a b)").bitcast(F32)
        wq_src = d_wqT.rearrange("(ic p) o -> p ic o", p=128)
        for ic in range(8):
            kb.do("sp", lambda h, ic=ic: h.dma_start(
                out=xwf, in_=wq_src[:, ic, :]),
                waits=[("wtr_3_7" if ic == 0 else f"wqc_{ic-1}")],
                inc=("skst0", 16), chk=f"wql_{ic}")
            kb.do("act", lambda h, ic=ic: h.activation(
                wa_wq[:, ic, :], xwf, AF.Copy),
                waits=[f"wql_{ic}", ("wkmm_3_7" if ic == 0 else None)],
                inc=("sact", 1), chk=f"wqc_{ic}")
        kb.chk["wq_ld"] = kb.chk["wqc_7"]
        emit_qkv_q()

    with nc.sbuf_tensor("KBUF", [128, 4, 1024], BF16) as KBUF, \
         nc.sbuf_tensor("VBUF", [128, 4, 1024], BF16) as VBUF:
        emit_wpair_load(1, 0, wa_w1, wa_w2, waits=["qmmall", "qall"])
        def _ln3_stats(qb):
            emit_stats_tile(2, 2 * qb, waits=[f"attev_{qb}"])
            emit_stats_tile(2, 2 * qb + 1)
        emit_attention(MSKC, KBUF, VBUF, on_evac=_ln3_stats)
    q_scope.close()

    with nc.sbuf_tensor("MEM1", [8, TCORE], BF16) as MEM1, \
         nc.sbuf_tensor("MPP1", [128, 2, TCORE], BF16) as MPP1, \
         nc.sbuf_tensor("HS1", [128, 2, 2, 1024], BF16) as HS1, \
         nc.sbuf_tensor("WB1", [128, 16384], BF16) as WB1:
        wb_any1 = WB1[:, 0:1]
        wb_w1b = lambda j: _view(WB1, wb_any1, j * 4096, [[512, 8], [1, 512]])
        wb_w2b = lambda j: _view(WB1, wb_any1, 8192 + j * 4096, [[1024, 4], [1, 1024]])
        emit_wpair_load(1, 1, wb_w1b, wb_w2b, waits=["AV_287", "MS_287", "P_287"])
        emit_prep(2, True, MEM1, lambda tau: None,
                  kb.chk["P_287"], kb.chk["attev_7"], stats_done=True)
        emit_ffn(1, 2, MEM1, MPP1, HS1, wb_w1b, wb_w2b,
                 kb.chk["AV_287"], kb.chk["attev_7"], kb.chk["P_287"])

    for t in range(NT):
        kb.do("sp", lambda h, t=t: h.dma_start(
            out=d_out.rearrange("(t p) d -> p t d", p=128)[:, t, :],
            in_=X[:, t, :]),
            waits=[f"f1_yev_3_{t // 2}"], inc=("sout", 16))
    kb.do("sp", lambda h: h.wait_ge(kb.sems["sout"], 16 * NT), waits=[])

    with nc.Block() as block:
        @block.tensor
        def _(h):
            for c in kb.stream["pe"]:
                c(h)

        @block.scalar
        def _(h):
            for c in kb.stream["act"]:
                c(h)

        @block.vector
        def _(h):
            for c in kb.stream["dve"]:
                c(h)

        @block.sync
        def _(h):
            for c in kb.stream["sp"]:
                c(h)

        @block.gpsimd
        def _(h):
            for c in kb.stream["pool"]:
                c(h)

    kb.es.close()
    return nc


_NC_CACHE = [None]


def kernel(L0_attn_X, L0_attn_Wq, L0_attn_bq, L0_attn_Wk, L0_attn_bk,
           L0_attn_Wv, L0_attn_bv, L0_ffn_gate_w, L0_ffn_w1, L0_ffn_b1,
           L0_ffn_w2, L0_ffn_b2):
    Xin = np.asarray(L0_attn_X)
    wqT = np.ascontiguousarray(np.asarray(L0_attn_Wq).T)
    wkT = np.ascontiguousarray(np.asarray(L0_attn_Wk).T)
    wvT = np.ascontiguousarray(np.asarray(L0_attn_Wv).T)
    bq = np.ascontiguousarray(np.asarray(L0_attn_bq).reshape(8, 128))
    bk = np.ascontiguousarray(np.asarray(L0_attn_bk).reshape(8, 128))
    bv = np.ascontiguousarray(np.asarray(L0_attn_bv).reshape(1, 1024))
    gw = np.ascontiguousarray(np.asarray(L0_ffn_gate_w))
    w1 = np.ascontiguousarray(np.asarray(L0_ffn_w1))
    b1 = np.ascontiguousarray(np.asarray(L0_ffn_b1))
    w2 = np.ascontiguousarray(np.asarray(L0_ffn_w2))
    b2 = np.ascontiguousarray(np.asarray(L0_ffn_b2))
    idf = np.eye(128, dtype=np.float32)
    eb = np.ascontiguousarray(
        (np.eye(8, dtype=np.float32)[:, :, None]
         * np.ones((1, 1, 128), np.float32)).reshape(8, 8 * 128))

    if _NC_CACHE[0] is None:
        _NC_CACHE[0] = build_program()
    nc = _NC_CACHE[0]

    in_maps = []
    for core in range(NCORE):
        s, c = core // 4, core % 4
        blocks = [Xin[s, 1024 * j + 256 * c:1024 * j + 256 * c + 256]
                  for j in range(8)]
        xs = np.ascontiguousarray(np.concatenate(blocks, axis=0))
        ki = np.arange(128)[:, None]
        mi = np.arange(256)[None, :]
        msk = np.stack([np.where(128 * j + ki > 256 * c + mi, -1e30, 0.0)
                        .astype(np.float32) for j in range(8)])
        in_maps.append({
            "x": xs, "wqT": wqT, "wkT": wkT, "wvT": wvT,
            "bq": bq, "bk": bk, "bv": bv, "gw": gw,
            "w1": w1, "b1": b1, "w2": w2, "b2": b2,
            "msk": np.ascontiguousarray(msk), "idf": idf, "eb": eb,
        })

    res = run_bass_kernel_spmd(nc, in_maps, core_ids=list(range(NCORE)))
    out = np.empty((2, 8192, 1024), dtype=np.float32)
    for core in range(NCORE):
        s, c = core // 4, core % 4
        o = res.results[core]["out"]
        for j in range(8):
            out[s, 1024 * j + 256 * c:1024 * j + 256 * c + 256] = \
                o[256 * j:256 * (j + 1)]
    return out


# version 18
# speedup vs baseline: 1.0583x; 1.0583x over previous
import contextlib
import numpy as np
import ml_dtypes
import concourse.bass as bass
import concourse.mybir as mybir
from concourse.bass_utils import run_bass_kernel_spmd

F32 = mybir.dt.float32
BF16 = mybir.dt.bfloat16
AF = mybir.ActivationFunctionType
ALU = mybir.AluOpType
AX = mybir.AxisListType

NCORE = 8
D = 1024
TCORE = 2048
NT = 16
NT2 = 8
NE = 8
HID = 512
NQ = 8
QSC = 1.0 / 32.0

ENGS = ("pe", "act", "dve", "sp", "pool")
DMA_SEMS = {"sx0", "sx1", "sx2", "sx3", "swc", "swa", "swb", "smsk",
             "skst0", "skst1", "skv0", "skv1", "skv2", "skv3", "skv4",
             "skv5", "svv0", "svv1", "svv2", "svv3", "svv4", "svv5", "sout", "scc0", "scc1", "scc2", "scc3",
             "skc0", "skc1", "skc2", "skc3"}


def _view(t_handle, any_ap, offset, pattern):
    return bass.AP(any_ap.tensor, offset, [[any_ap.ap[0][0], 128]] + pattern)


class KB:
    def __init__(self, nc):
        self.nc = nc
        self.stream = {e: [] for e in ENGS}
        self.cnt = {}
        self.chk = {}
        self.sems = {}
        self.last_wait = {e: {} for e in ENGS}
        self.es = contextlib.ExitStack()

    def sem(self, name):
        self.sems[name] = self.es.enter_context(self.nc.semaphore(name))
        self.cnt[name] = 0

    def do(self, eng, fn, waits=(), inc=None, chk=None):
        wl = []
        for w in waits:
            if w is None:
                continue
            if isinstance(w, str):
                w = self.chk[w]
            s, v, prod = w
            if self.last_wait[eng].get(s, -1) >= v:
                continue
            self.last_wait[eng][s] = v
            wl.append((s, v))
        if inc is not None:
            s, k = inc
            self.cnt[s] += k
            if chk is not None:
                prod = "dma" if s in DMA_SEMS else eng
                self.chk[chk] = (s, self.cnt[s], prod)
        sems = self.sems

        def closure(h, fn=fn, wl=wl, inc=inc):
            for s, v in wl:
                h.wait_ge(sems[s], v)
            r = fn(h)
            if inc is not None:
                r.then_inc(sems[inc[0]], inc[1])
        self.stream[eng].append(closure)

    def waitonly(self, eng, waits):
        self.do(eng, lambda h: None, waits=waits)


def build_program():
    nc = bass.Bass("TRN2", target_bir_lowering=False)
    kb = KB(nc)
    E = kb.es.enter_context

    d_x = nc.dram_tensor("x", [TCORE, D], F32, kind="ExternalInput").ap()
    d_wqT = nc.dram_tensor("wqT", [D, D], F32, kind="ExternalInput").ap()
    d_wkT = nc.dram_tensor("wkT", [D, D], F32, kind="ExternalInput").ap()
    d_wvT = nc.dram_tensor("wvT", [D, D], F32, kind="ExternalInput").ap()
    d_bq = nc.dram_tensor("bq", [8, 128], F32, kind="ExternalInput").ap()
    d_bk = nc.dram_tensor("bk", [8, 128], F32, kind="ExternalInput").ap()
    d_bv = nc.dram_tensor("bv", [1, D], F32, kind="ExternalInput").ap()
    d_gw = nc.dram_tensor("gw", [D, NE], F32, kind="ExternalInput").ap()
    d_w1 = nc.dram_tensor("w1", [NE, D, HID], F32, kind="ExternalInput").ap()
    d_b1 = nc.dram_tensor("b1", [NE, HID], F32, kind="ExternalInput").ap()
    d_w2 = nc.dram_tensor("w2", [NE, HID, D], F32, kind="ExternalInput").ap()
    d_b2 = nc.dram_tensor("b2", [NE, D], F32, kind="ExternalInput").ap()
    d_msk = nc.dram_tensor("msk", [8, 128, 256], F32, kind="ExternalInput").ap()
    d_idf = nc.dram_tensor("idf", [128, 128], F32, kind="ExternalInput").ap()
    d_eb = nc.dram_tensor("eb", [8, 8 * 128], F32, kind="ExternalInput").ap()
    d_out = nc.dram_tensor("out", [TCORE, D], F32, kind="ExternalOutput").ap()

    t_kvself = nc.dram_tensor("kvself", [2 * TCORE, D], BF16, kind="Internal")
    t_kvgath = nc.dram_tensor("kvgath", [8 * TCORE, D], BF16, kind="Internal")
    d_kvself = t_kvself.ap()
    d_kvgath = t_kvgath.ap()

    for s in ("sx0", "sx1", "sx2", "sx3", "swc", "swa", "swb", "smsk",
              "skst0", "skst1",
              "skv0", "skv1", "skv2", "skv3", "skv4", "skv5",
              "svv0", "svv1", "svv2", "svv3", "svv4", "svv5",
              "scc0", "scc1", "scc2", "scc3", "sout",
              "skc0", "skc1", "skc2", "skc3",
              "spe", "sdve", "sact", "spool"):
        kb.sem(s)

    X = E(nc.sbuf_tensor("X", [128, NT, D], F32))
    XHF = E(nc.sbuf_tensor("XHF", [128, 8, TCORE], BF16))
    XHT = E(nc.sbuf_tensor("XHT", [128, 2, D], BF16))
    WA = E(nc.sbuf_tensor("WA", [128, 16384], BF16))
    IDF = E(nc.sbuf_tensor("IDF", [128, 128], F32))
    IDB = E(nc.sbuf_tensor("IDB", [128, 128], BF16))
    GWS = E(nc.sbuf_tensor("GWS", [128, 8, 8], BF16))
    B1T = E(nc.sbuf_tensor("B1T", [128, 4, 8], F32))
    B2B = E(nc.sbuf_tensor("B2B", [8, D], BF16))
    BQT = E(nc.sbuf_tensor("BQT", [128, 8], F32))
    BKT = E(nc.sbuf_tensor("BKT", [128, 8], F32))
    ONESR = E(nc.sbuf_tensor("ONESR", [1, 128], BF16))
    EB = E(nc.sbuf_tensor("EB", [8, 8, 128], BF16))
    ONESC = E(nc.sbuf_tensor("ONESC", [128, 1], BF16))
    CONSTS = E(nc.sbuf_tensor("CONSTS", [128, 8], F32))
    EPSC = CONSTS[:, 0:1]
    CNID = CONSTS[:, 1:2]
    CONE = CONSTS[:, 2:3]
    CHALF = CONSTS[:, 3:4]
    CBIG = CONSTS[:, 4:5]
    CQS = CONSTS[:, 5:6]
    CZERO = CONSTS[:, 6:7]
    CPD = CONSTS[:, 7:8]
    STAT = E(nc.sbuf_tensor("STAT", [128, 7, NT], F32))
    GA = E(nc.sbuf_tensor("GA", [128, NT, 8], F32))
    MTM = E(nc.sbuf_tensor("MTM", [128, NT, 8], F32))
    GST = E(nc.sbuf_tensor("GST", [128, 6, NT], F32))
    GEXP3 = E(nc.sbuf_tensor("GEXP3", [128, NT, 8], F32))
    LINV = E(nc.sbuf_tensor("LINV", [128, 2], F32))
    PTS = E(nc.sbuf_tensor("PTS", [128, 2, 256], BF16))
    JUNK = E(nc.sbuf_tensor("JUNK", [128, D], BF16))

    wa_any = WA[:, 0:1]

    def wa_w1(j):
        return _view(WA, wa_any, j * 4096, [[512, 8], [1, 512]])

    def wa_w2(j):
        return _view(WA, wa_any, 8192 + j * 4096, [[1024, 4], [1, 1024]])

    wa_wq = _view(WA, wa_any, 0, [[1024, 8], [1, 1024]])
    wa_wk = _view(WA, wa_any, 8192, [[1024, 8], [1, 1024]])

    def g_dma(dst, src, chkname=None, waits=(), sem="swc"):
        kb.do("pool", lambda h, d=dst, s=src: h.dma_start(out=d, in_=s),
              waits=waits, inc=(sem, 16), chk=chkname)

    kb.do("pool", lambda h: h.memset(ONESR[:, :], 1.0), inc=("spool", 1))
    kb.do("pool", lambda h: h.memset(ONESC[:, :], 1.0), inc=("spool", 1))
    for col, val in ((0, 1e-5), (1, -1.0 / D), (2, 1.0), (3, 0.5),
                     (4, -1e30), (5, QSC), (6, 0.0), (7, 1.0 / D)):
        kb.do("pool", lambda h, c=col, v=val: h.memset(CONSTS[:, c:c + 1], v),
              inc=("spool", 1))
    kb.chk["memsets"] = ("spool", kb.cnt["spool"], "pool")

    g_dma(IDF[:, :], d_idf)
    g_dma(IDB[:, :], d_idf)
    g_dma(GWS[:, :, :], d_gw.rearrange("(ic p) e -> p ic e", p=128))
    g_dma(B2B[:, :], d_b2)
    g_dma(EB[:, :, :], d_eb.rearrange("k (e m) -> k e m", e=8))

    with nc.sbuf_tensor("B1SB", [8, HID], F32) as B1SB, \
         nc.sbuf_tensor("BQSB", [8, 128], F32) as BQSB, \
         nc.sbuf_tensor("BKSB", [8, 128], F32) as BKSB:
        g_dma(B1SB[:, :], d_b1)
        g_dma(BQSB[:, :], d_bq)
        g_dma(BKSB[:, :], d_bk, "consts_all")
        b1sb_ap = B1SB
        bqsb_ap = BQSB
        bksb_ap = BKSB

    for g in range(4):
        kb.do("sp", lambda h, g=g: h.dma_start(
            out=X[:, 4 * g:4 * g + 4, :],
            in_=d_x.rearrange("(t p) d -> p t d", p=128)[:, 4 * g:4 * g + 4, :]),
            inc=(f"sx{g}", 16), chk=f"xg{g}")

    def emit_wpair_load(fidx, p, slotview_w1, slotview_w2, waits):
        sem = "swa" if p % 2 == 0 else "swb"
        for j in range(2):
            e = 2 * p + j
            g_dma(slotview_w1(j),
                  d_w1[e].rearrange("(ic p) h -> p ic h", p=128),
                  None, waits=waits, sem=sem)
            g_dma(slotview_w2(j),
                  d_w2[e].rearrange("(hc p) d -> p hc d", p=128),
                  (f"wload_{fidx}_{p}" if j == 1 else None), waits=waits, sem=sem)

    emit_wpair_load(0, 0, wa_w1, wa_w2, ())

    def emit_bvec_transposes(PBF):
        for hc in range(4):
            kb.do("pe", lambda h, hc=hc: h.transpose(
                PBF[:, hc * 8:hc * 8 + 8], b1sb_ap[0:8, hc * 128:(hc + 1) * 128],
                IDF[0:8, 0:8]),
                waits=["consts_all", "consts_all"],
                inc=("spe", 1), chk=f"b1tr{hc}")
        kb.do("pe", lambda h: h.transpose(
            PBF[:, 32:40], bqsb_ap[0:8, :], IDF[0:8, 0:8]),
            waits=["consts_all"], inc=("spe", 1), chk="bqtr")
        kb.do("pe", lambda h: h.transpose(
            PBF[:, 40:48], bksb_ap[0:8, :], IDF[0:8, 0:8]),
            waits=["consts_all"], inc=("spe", 1), chk="bktr")
        kb.do("dve", lambda h: h.tensor_copy(
            B1T[:, :, :], PBF[:, 0:32].rearrange("p (hc e) -> p hc e", hc=4)),
            waits=["b1tr3"], inc=("sdve", 1))
        kb.do("dve", lambda h: h.tensor_copy(BQT[:, :], PBF[:, 32:40]),
              waits=["bqtr"], inc=("sdve", 1))
        kb.do("dve", lambda h: h.tensor_copy(BKT[:, :], PBF[:, 40:48]),
              waits=["bktr"], inc=("sdve", 1), chk="bvecs")

    junk_chain = [None]

    def emit_stats_tile(stage, tau, waits=()):
        sums = STAT[:, 0, :]
        sumsq = STAT[:, 2, :]
        kb.do("dve", lambda h, t=tau: h.reduce_sum(
            sums[:, t:t + 1], X[:, t, :], AX.X),
            waits=list(waits) + ["memsets"],
            inc=("sdve", 1), chk=f"sum{stage}_{tau}")
        kb.do("dve", lambda h, t=tau: h.scalar_tensor_tensor(
            JUNK[:, :], X[:, t, :], CZERO, X[:, t, :],
            ALU.add, ALU.mult, accum_out=sumsq[:, t:t + 1]),
            waits=[junk_chain[0]],
            inc=("sdve", 1), chk=f"var{stage}_{tau}")
        junk_chain[0] = f"var{stage}_{tau}"

    def emit_prep(stage, gating, MEM, xwait, entry_pe, entry_dve,
                  stats_done=False):
        sums = STAT[:, 0, :]
        nmean = STAT[:, 1, :]
        sumsq = STAT[:, 2, :]
        m2 = STAT[:, 3, :]
        var_ = STAT[:, 4, :]
        std = STAT[:, 5, :]
        rstd = STAT[:, 6, :]

        with nc.psum_tensor("PT0_%d" % stage, [128, 512], BF16) as PT0, \
             nc.psum_tensor("PT1_%d" % stage, [128, 512], BF16) as PT1, \
             nc.psum_tensor("PG0_%d" % stage, [128, 8], F32) as PG0, \
             nc.psum_tensor("PG1_%d" % stage, [128, 8], F32) as PG1, \
             nc.psum_tensor("PBF_%d" % stage, [128, 512], F32) as PBF:
            PT = (PT0, PT1)
            PG = (PG0, PG1)
            if stage == 0:
                emit_bvec_transposes(PBF)

            if not stats_done:
                for tau in range(NT):
                    emit_stats_tile(stage, tau, waits=[xwait(tau)])
            gsz = 4 if stage == 0 else NT
            for g0 in range(0, NT, gsz):
                S4 = slice(g0, g0 + gsz)
                gn = g0 // gsz
                kb.do("dve", lambda h, S4=S4: h.tensor_scalar(
                    nmean[:, S4], sums[:, S4], CNID, None, ALU.mult),
                    waits=[f"sum{stage}_{g0+gsz-1}", f"var{stage}_{g0+gsz-1}"],
                    inc=("sdve", 1), chk=f"nmean{stage}g{gn}")
                kb.do("dve", lambda h, S4=S4: h.tensor_mul(
                    m2[:, S4], nmean[:, S4], nmean[:, S4]),
                    waits=[f"nmean{stage}g{gn}"], inc=("sdve", 1),
                    chk=f"m2{stage}g{gn}")
                kb.do("dve", lambda h, S4=S4: h.scalar_tensor_tensor(
                    var_[:, S4], sumsq[:, S4], CPD, m2[:, S4],
                    ALU.mult, ALU.subtract),
                    waits=[f"m2{stage}g{gn}"], inc=("sdve", 1),
                    chk=f"varb{stage}g{gn}")
                kb.do("act", lambda h, S4=S4: h.activation(
                    std[:, S4], var_[:, S4], AF.Sqrt, bias=EPSC, scale=1.0),
                    waits=[f"varb{stage}g{gn}", "memsets"],
                    inc=("sact", 1), chk=f"std{stage}g{gn}")
                kb.do("dve", lambda h, S4=S4: h.reciprocal(
                    rstd[:, S4], std[:, S4]),
                    waits=[f"std{stage}g{gn}"], inc=("sdve", 1),
                    chk=f"rstd{stage}g{gn}")

            for tau in range(NT):
                par = tau % 2
                kb.do("dve", lambda h, t=tau, par=par: h.tensor_scalar(
                    XHT[:, par, :], X[:, t, :], nmean[:, t:t + 1],
                    rstd[:, t:t + 1], ALU.add, ALU.mult),
                    waits=[f"rstd{stage}g{tau // gsz}",
                           f"tr{stage}_{tau-2}_1" if tau >= 2 else None],
                    inc=("sdve", 1), chk=f"app{stage}_{tau}")
                for grp in range(2):
                    for k in range(4):
                        dc = grp * 4 + k
                        kb.do("pe", lambda h, t=tau, par=par, dc=dc, grp=grp: h.transpose(
                            PT[(2 * t + grp) % 2][:, (dc % 4) * 128:(dc % 4 + 1) * 128],
                            XHT[:, par, dc * 128:(dc + 1) * 128], IDB[:, :]),
                            waits=[f"app{stage}_{tau}", "consts_all",
                                   (f"trev{stage}_{tau-1}_{grp}" if tau >= 1 else entry_dve),
                                   (entry_pe if tau == 0 and dc == 0 else None)],
                            inc=("spe", 1),
                            chk=(f"tr{stage}_{tau}_{grp}" if k == 3 else None))
                    kb.do("dve", lambda h, t=tau, grp=grp: h.tensor_copy(
                        XHF[:, 4 * grp:4 * grp + 4, t * 128:(t + 1) * 128],
                        PT[(2 * t + grp) % 2][:, 0:512].rearrange(
                            "p (a b) -> p a b", a=4)),
                        waits=[f"tr{stage}_{tau}_{grp}"],
                        inc=("sdve", 1), chk=f"trev{stage}_{tau}_{grp}")
                if gating:
                    for ic in range(8):
                        kb.do("pe", lambda h, t=tau, ic=ic, par=par: h.matmul(
                            PG[par][:, :], XHF[:, ic, t * 128:(t + 1) * 128],
                            GWS[:, ic, :], start=(ic == 0), stop=(ic == 7)),
                            waits=[f"trev{stage}_{tau}_1", "consts_all",
                                   f"gev{stage}_{tau-2}" if tau >= 2 else None],
                            inc=(("spe", 1) if ic == 7 else None),
                            chk=(f"g{stage}_{tau}" if ic == 7 else None))
                    kb.do("dve", lambda h, t=tau, par=par: h.tensor_copy(
                        GA[:, t, :], PG[par][:, :]),
                        waits=[f"g{stage}_{tau}"],
                        inc=("sdve", 1), chk=f"gev{stage}_{tau}")

            kb.chk[f"prepdone{stage}"] = kb.chk[f"trev{stage}_{NT-1}_1"]

            if gating:
                g_s1 = GST[:, 0, :]
                g_z = GST[:, 1, :]
                g_s2 = GST[:, 2, :]
                g_d = GST[:, 3, :]
                g_e2 = GST[:, 4, :]
                g_msc = GST[:, 5, :]

                ggs = 4 if stage == 0 else NT
                for gg0 in range(0, NT, ggs):
                    GS = slice(gg0, gg0 + ggs)
                    s1g = g_s1[:, GS]
                    zg = g_z[:, GS]
                    s2g = g_s2[:, GS]
                    dg = g_d[:, GS]
                    e2g = g_e2[:, GS]
                    mscg = g_msc[:, GS]
                    gag = GA[:, GS, :]
                    geg = GEXP3[:, GS, :]
                    mtg = MTM[:, GS, :]

                    def bc3(ap2d, n=ggs):
                        return bass.AP(ap2d.tensor, ap2d.offset,
                                       [[ap2d.ap[0][0], 128], [1, n], [0, 8]])

                    nstp = [0]

                    def gb(fn, waits=(), engine="dve", gg0=gg0, nstp=nstp):
                        nstp[0] += 1
                        name = f"gb{stage}_{gg0}_{nstp[0]}"
                        prev = (f"gb{stage}_{gg0}_{nstp[0]-1}"
                                if nstp[0] > 1 else None)
                        kb.do(engine, fn, waits=list(waits) + [prev],
                              inc=(("sdve", 1) if engine == "dve"
                                   else ("sact", 1)),
                              chk=name)
                        return name

                    gb(lambda h, a=s1g, b=gag: h.reduce_max(a, b, AX.X),
                       waits=[f"gev{stage}_{gg0+ggs-1}"])
                    gb(lambda h, a=geg, b=gag, c=bc3(s1g): h.tensor_tensor(
                        a, b, c, ALU.subtract))
                    gb(lambda h, a=mtg, b=geg: h.activation(a, b, AF.Exp),
                       engine="act")
                    gb(lambda h, a=zg, b=mtg: h.reduce_sum(a, b, AX.X))
                    gb(lambda h, a=geg, b=gag, c=bc3(s1g): h.tensor_tensor(
                        a, b, c, ALU.is_ge))
                    gb(lambda h, a=mtg, b=geg, c=gag: h.scalar_tensor_tensor(
                        a, b, CBIG, c, ALU.mult, ALU.add))
                    gb(lambda h, a=s2g, b=mtg: h.reduce_max(a, b, AX.X))
                    gb(lambda h, a=dg, b=s2g, c=s1g: h.tensor_sub(a, b, c))
                    gb(lambda h, a=e2g, b=dg: h.activation(a, b, AF.Exp),
                       engine="act")
                    gb(lambda h, a=dg, b=zg: h.reciprocal(a, b))
                    gb(lambda h, a=e2g: h.tensor_scalar(a, a, CONE, None,
                                                        ALU.add))
                    gb(lambda h, a=mscg, b=e2g, c=dg: h.tensor_mul(a, b, c))
                    gb(lambda h, a=mscg: h.tensor_scalar(a, a, CHALF, None,
                                                         ALU.mult))
                    gb(lambda h, a=geg, b=gag, c=bc3(s2g): h.tensor_tensor(
                        a, b, c, ALU.is_ge))
                    last = gb(lambda h, a=mtg, b=geg, c=bc3(mscg):
                              h.tensor_tensor(a, b, c, ALU.mult))
                    for tau in range(gg0, gg0 + ggs):
                        kb.chk[f"mtm_{stage}_{tau}"] = kb.chk[last]

                for g4 in range(4):
                    for k in range(4):
                        tau = g4 * 4 + k
                        kb.do("pe", lambda h, tau=tau, k=k: h.transpose(
                            PBF[0:8, k * 128:(k + 1) * 128], MTM[:, tau, :],
                            IDF[:, :]),
                            waits=[f"mtm_{stage}_{tau}",
                                   f"memev_{stage}_{g4-1}" if g4 >= 1 else ("bvecs" if stage == 0 else None)],
                            inc=("spe", 1),
                            chk=(f"mtr_{stage}_{g4}" if k == 3 else None))
                    kb.do("dve", lambda h, g4=g4: h.tensor_copy(
                        MEM[0:8, g4 * 512:(g4 + 1) * 512], PBF[0:8, 0:512]),
                        waits=[f"mtr_{stage}_{g4}"],
                        inc=("sdve", 1), chk=f"memev_{stage}_{g4}")
                kb.chk[f"memdone{stage}"] = kb.chk[f"memev_{stage}_3"]
                for g4 in range(4):
                    kb.chk[f"memdone{stage}_g{g4}"] = kb.chk[f"memev_{stage}_{g4}"]

    def emit_ffn(fidx, stage, MEM, MPP, HS, wb_w1, wb_w2, entry_pe, entry_dve,
                 entry_gelu, on_yev=None):
        with nc.psum_tensor("PH0_%d" % fidx, [128, 1024], F32) as PH0, \
             nc.psum_tensor("PH1_%d" % fidx, [128, 1024], F32) as PH1, \
             nc.psum_tensor("PY0_%d" % fidx, [128, 1024], F32) as PY0, \
             nc.psum_tensor("PY1_%d" % fidx, [128, 1024], F32) as PY1:
            PH = (PH0, PH1)
            PY = (PY0, PY1)
            for p in range(4):
                w1v = wa_w1 if p % 2 == 0 else wb_w1
                w2v = wa_w2 if p % 2 == 0 else wb_w2
                for j in range(2):
                    e = 2 * p + j
                    for qr in range(4):
                        hb = qr % 2
                        kb.do("pe", lambda h, e=e, qr=qr, hb=hb: h.matmul(
                            PH0[:, hb * 512:(hb + 1) * 512], EB[0:8, e, :],
                            MEM[0:8, qr * 512:(qr + 1) * 512],
                            start=True, stop=True),
                            waits=[f"memdone{stage}_g{qr}", "consts_all", kb.chk.get("memsets"),
                                   (f"f{fidx}_gelu_{p-1}_7_0" if p >= 1 else entry_gelu),
                                   (entry_pe if p == 0 and j == 0 and qr == 0 else None),
                                   (f"mbev_{fidx}_{p}_{j}_{qr-2}" if qr >= 2 else
                                    (f"mbev_{fidx}_{p}_{j-1}_{qr+2}" if j >= 1 else
                                     (f"mbev_{fidx}_{p-1}_1_{qr+2}" if p >= 1 else None)))],
                            inc=("spe", 1), chk=f"mb_{fidx}_{p}_{j}_{qr}")
                        kb.do("dve", lambda h, j=j, qr=qr, hb=hb: h.tensor_copy(
                            MPP[:, j, qr * 512:(qr + 1) * 512],
                            PH0[:, hb * 512:(hb + 1) * 512]),
                            waits=[f"mb_{fidx}_{p}_{j}_{qr}",
                                   (f"f{fidx}_hm_{p-1}_7_{j}" if p >= 1 else None)],
                            inc=("sdve", 1), chk=f"mbev_{fidx}_{p}_{j}_{qr}")

                def emit_h(t, j):
                    e = 2 * p + j
                    wchk = f"wload_{fidx}_{p}"
                    for hc in range(4):
                        for ic in range(8):
                            last = (hc == 3 and ic == 7)
                            kb.do("pe", lambda h, t=t, j=j, hc=hc, ic=ic, w1v=w1v:
                                  h.matmul(PH[j][:, hc * 256:(hc + 1) * 256],
                                           w1v(j)[:, ic, hc * 128:(hc + 1) * 128],
                                           XHF[:, ic, t * 256:(t + 1) * 256],
                                           start=(ic == 0),
                                           stop=(ic == 7)),
                                  waits=[wchk, f"trev{stage}_{2*t+1}_1",
                                         f"mbev_{fidx}_{p}_1_3" if j == 0 and t == 0 else None,
                                         (f"f{fidx}_gelu_{p}_{t-1}_{j}" if t >= 1 else
                                          (f"f{fidx}_gelu_{p-1}_7_{j}" if p >= 1 else None))],
                                  inc=(("spe", 1) if last else None),
                                  chk=(f"f{fidx}_h_{p}_{t}_{j}" if last else None))
                    tp = t % 2
                    if t >= 2:
                        hswar = f"f{fidx}_y_{p}_{t-2}"
                    elif p >= 1:
                        hswar = f"f{fidx}_y_{p-1}_{6 + tp}"
                    else:
                        hswar = entry_pe if fidx == 1 else None
                    for hc in range(4):
                        kb.do("act", lambda h, t=t, j=j, hc=hc, e=e, tp=tp: h.activation(
                            HS[:, j, tp, hc * 256:(hc + 1) * 256],
                            PH[j][:, hc * 256:(hc + 1) * 256],
                            AF.Gelu, bias=B1T[:, hc, e:e + 1]),
                            waits=[f"f{fidx}_h_{p}_{t}_{j}", "bvecs", hswar],
                            inc=("sact", 1),
                            chk=(f"f{fidx}_gelu_{p}_{t}_{j}" if hc == 3 else None))
                    mpslice = MPP[:, j, t * 256:(t + 1) * 256]
                    mp_b = bass.AP(mpslice.tensor, mpslice.offset,
                                   [[mpslice.ap[0][0], 128], [0, 4], [1, 256]])
                    hview = HS[:, j, tp, :].rearrange("p (a b) -> p a b", a=4)
                    kb.do("dve", lambda h, hview=hview, mp_b=mp_b: h.tensor_tensor(
                        hview, hview, mp_b, ALU.mult),
                        waits=[f"f{fidx}_gelu_{p}_{t}_{j}", f"mbev_{fidx}_{p}_{j}_3"],
                        inc=("sdve", 1), chk=f"f{fidx}_hm_{p}_{t}_{j}")

                def emit_y(t):
                    if p == 0:
                        for ts in range(2):
                            for dn in range(2):
                                kb.do("pe", lambda h, t=t, ts=ts, dn=dn: h.matmul(
                                    PY[ts][:, dn * 512:(dn + 1) * 512],
                                    MEM[0:8, (2 * t + ts) * 128:(2 * t + ts + 1) * 128],
                                    B2B[0:8, dn * 512:(dn + 1) * 512],
                                    start=True, stop=False, skip_group_check=True),
                                    waits=[f"memdone{stage}_g{(2*t+ts)//4}", "consts_all",
                                           (f"f{fidx}_yev_{p}_{t-1}" if t >= 1 else entry_dve)],
                                    inc=None)
                    for j in range(2):
                        wchk = f"wload_{fidx}_{p}"
                        for ts in range(2):
                            for hc in range(4):
                                for dn in range(2):
                                    first = (p > 0 and j == 0 and hc == 0)
                                    last = (j == 1 and ts == 1 and hc == 3 and dn == 1)
                                    kb.do("pe", lambda h, t=t, j=j, ts=ts, hc=hc, dn=dn, w2v=w2v, st=first:
                                          h.matmul(PY[ts][:, dn * 512:(dn + 1) * 512],
                                                   HS[:, j, t % 2, hc * 256 + ts * 128:hc * 256 + (ts + 1) * 128],
                                                   w2v(j)[:, hc, dn * 512:(dn + 1) * 512],
                                                   start=st,
                                                   stop=(j == 1 and hc == 3),
                                                   skip_group_check=True),
                                          waits=[wchk, f"f{fidx}_hm_{p}_{t}_{j}",
                                                 (f"f{fidx}_yev_{p}_{t-1}" if (p > 0 and t >= 1) else None),
                                                 (f"f{fidx}_yev_{p-1}_7" if (p > 0 and t == 0) else None)],
                                          inc=(("spe", 1) if last else None),
                                          chk=(f"f{fidx}_y_{p}_{t}" if last else None))
                    for ts in range(2):
                        kb.do("dve", lambda h, t=t, ts=ts: h.tensor_tensor(
                            X[:, 2 * t + ts, :], PY[ts][:, :], X[:, 2 * t + ts, :],
                            ALU.add),
                            waits=[f"f{fidx}_y_{p}_{t}",
                                   (f"f{fidx}_yev_{p-1}_{t}" if p >= 1 else None)],
                            inc=("sdve", 1),
                            chk=(f"f{fidx}_yev_{p}_{t}" if ts == 1 else None))
                    if p == 3 and on_yev is not None:
                        on_yev(t)

                emit_h(0, 0)
                emit_h(0, 1)
                for t in range(1, NT2):
                    emit_h(t, 0)
                    emit_h(t, 1)
                    emit_y(t - 1)
                emit_y(NT2 - 1)

                if p + 2 <= 3:
                    tgt_w1 = wa_w1 if (p + 2) % 2 == 0 else wb_w1
                    tgt_w2 = wa_w2 if (p + 2) % 2 == 0 else wb_w2
                    emit_wpair_load(fidx, p + 2,
                                    lambda j, v=tgt_w1: v(j),
                                    lambda j, v=tgt_w2: v(j),
                                    waits=[f"f{fidx}_y_{p}_{NT2-1}"])
            kb.chk[f"ffndone{fidx}"] = kb.chk[f"f{fidx}_yev_3_{NT2-1}"]
            kb.chk[f"ffngelu{fidx}"] = kb.chk[f"f{fidx}_gelu_3_{NT2-1}_1"]

    def emit_qkv(WVV, KST, VST, BVR):
        with nc.psum_tensor("PQ0", [128, 512], F32) as PQ0, \
             nc.psum_tensor("PQ1", [128, 512], F32) as PQ1:
            PQ = (PQ0, PQ1)
            pq = 0
            pq_chk = {}

            def pq_wait():
                return pq_chk.get(pq % 2)

            for tb in range(4):
                for dc in range(8):
                    par = pq % 2
                    for ic in range(8):
                        kb.do("pe", lambda h, dc=dc, tb=tb, ic=ic, par=par: h.matmul(
                            PQ[par][:, :], wa_wk[:, ic, dc * 128:(dc + 1) * 128],
                            XHF[:, ic, tb * 512:(tb + 1) * 512],
                            start=(ic == 0), stop=(ic == 7)),
                            waits=["wqk_ld", "prepdone1", "ffngelu0", "ffndone0",
                                   pq_wait()],
                            inc=(("spe", 1) if ic == 7 else None),
                            chk=(f"kmm_{tb}_{dc}" if ic == 7 else None))
                    kb.do("dve", lambda h, dc=dc, par=par: h.tensor_scalar(
                        KST[:, dc, :], PQ[par][:, :], BKT[:, dc:dc + 1], None,
                        ALU.add),
                        waits=[f"kmm_{tb}_{dc}", "bvecs",
                               ((f"skc{tb-1}", 32, "dma") if (tb >= 1 and dc == 0)
                                else None)],
                        inc=("sdve", 1), chk=f"kev_{tb}_{dc}")
                    pq_chk[par] = f"kev_{tb}_{dc}"
                    pq += 1
                dst = d_kvself[tb * 1024:tb * 1024 + 512, :].rearrange(
                    "(b p) (c t) -> p c b t", p=128, c=8)
                kb.do("sp", lambda h, dst=dst, tb=tb: h.dma_start(
                    out=dst,
                    in_=KST[:, :, :].rearrange("p c (b t) -> p c b t", b=4)),
                    waits=[f"kev_{tb}_7"], inc=(f"skc{tb}", 16),
                    chk=f"kst_{tb}")
                for k4 in range(8):
                    ts16 = 4 * tb + k4 // 2
                    dn = k4 % 2
                    par = pq % 2
                    for ic in range(8):
                        kb.do("pe", lambda h, ts16=ts16, dn=dn, ic=ic, par=par:
                              h.matmul(PQ[par][:, :],
                                       XHF[:, ic, ts16 * 128:(ts16 + 1) * 128],
                                       WVV[:, ic, dn * 512:(dn + 1) * 512],
                                       start=(ic == 0), stop=False),
                              waits=["wv_ld", pq_wait()],
                              inc=None)
                    kb.do("pe", lambda h, dn=dn, par=par, BVR=BVR: h.matmul(
                        PQ[par][:, :], ONESR[:, :], BVR[0:1, dn * 512:(dn + 1) * 512],
                        start=False, stop=True),
                        waits=["memsets"],
                        inc=("spe", 1), chk=f"vmm_{tb}_{k4}")
                    kb.do("dve", lambda h, par=par, k4=k4: h.tensor_copy(
                        VST[:, k4, :], PQ[par][:, :]),
                        waits=[f"vmm_{tb}_{k4}",
                               ((f"skc{tb-1}", 32, "dma") if (tb >= 1 and k4 == 0)
                                else None)],
                        inc=("sdve", 1), chk=f"vev_{tb}_{k4}")
                    pq_chk[par] = f"vev_{tb}_{k4}"
                    pq += 1
                dst = d_kvself[tb * 1024 + 512:(tb + 1) * 1024, :].rearrange(
                    "(b p) (n d) -> p b n d", p=128, n=2)
                kb.do("sp", lambda h, dst=dst, tb=tb: h.dma_start(
                    out=dst,
                    in_=VST[:, :, :].rearrange("p (b n) d -> p b n d", n=2)),
                    waits=[f"vev_{tb}_7"], inc=(f"skc{tb}", 16),
                    chk=f"vst_{tb}")
                kb.do("pool", lambda h, tb=tb: h.collective_compute(
                    "AllGather", ALU.bypass,
                    replica_groups=[[0, 1, 2, 3], [4, 5, 6, 7]],
                    ins=[d_kvself[tb * 1024:(tb + 1) * 1024, :].bitcast(F32).opt()],
                    outs=[d_kvgath[tb * 4096:(tb + 1) * 4096, :].bitcast(F32).opt()]),
                    waits=[(f"skc{tb}", 32, "dma")],
                    inc=(f"scc{tb}", 1), chk=f"cc{tb}")

            for tb in range(4):
                for dc in range(8):
                    par = pq % 2
                    for ic in range(8):
                        kb.do("pe", lambda h, dc=dc, tb=tb, ic=ic, par=par: h.matmul(
                            PQ[par][:, :], wa_wq[:, ic, dc * 128:(dc + 1) * 128],
                            XHF[:, ic, tb * 512:(tb + 1) * 512],
                            start=(ic == 0), stop=(ic == 7)),
                            waits=["wqk_ld", pq_wait()],
                            inc=(("spe", 1) if ic == 7 else None),
                            chk=(f"qmm_{tb}_{dc}" if ic == 7 else None))
                    kb.do("dve", lambda h, dc=dc, tb=tb, par=par: h.tensor_scalar(
                        Q[:, dc, tb * 512:(tb + 1) * 512], PQ[par][:, :],
                        BQT[:, dc:dc + 1], CQS, ALU.add, ALU.mult),
                        waits=[f"qmm_{tb}_{dc}", "bvecs", "memsets"],
                        inc=("sdve", 1),
                        chk=f"qev_{tb}_{dc}")
                    pq_chk[par] = f"qev_{tb}_{dc}"
                    pq += 1
                kb.chk[f"qevtb{tb}"] = kb.chk[f"qev_{tb}_7"]
            kb.chk["qmmall"] = kb.chk["qmm_3_7"]
            kb.chk["qall"] = kb.chk["qev_3_7"]

    def emit_attention(MSKC, KBUF, VBUF, on_evac=None):
        with nc.psum_tensor("PS0", [128, 256], F32) as PS0, \
             nc.psum_tensor("PS1", [128, 256], F32) as PS1, \
             nc.psum_tensor("PO0", [128, 1024], F32) as PO0, \
             nc.psum_tensor("PO1", [128, 1024], F32) as PO1, \
             nc.psum_tensor("PLA", [128, 1], F32) as PLA, \
             nc.psum_tensor("PLB", [128, 1], F32) as PLB:
            PS = (PS0, PS1)
            PO = (PO0, PO1)
            steps = []
            for qb in range(NQ):
                kn = 8 * qb + 8
                for t in range(kn):
                    steps.append((qb, t, kn))
            nsteps = len(steps)

            def kv_load(i):
                qb, t, kn = steps[i]
                g = t
                v_ = (g // 2) % 4
                j2 = g // 8
                u_ = 2 * j2 + (g % 2)
                tbc = j2 // 2
                bp = u_ - 4 * tbc
                krow = 4096 * tbc + 1024 * v_ + 128 * bp
                vrow = krow + 512
                buf = i % 6
                w = [f"cc{tbc}",
                     (f"AV_{i-6}" if i >= 6 else "qmmall"),
                     ("qall" if i < 6 else None),
                     (("skc3", 32, "dma") if i < 6 else None)]
                kb.do("sp", lambda h, krow=krow, buf=buf: h.dma_start(
                    out=KBUF[:, buf, :],
                    in_=d_kvgath[krow:krow + 128, :]),
                    waits=w, inc=(f"skv{buf}", 16), chk=f"kvldK_{i}")
                kb.do("sp", lambda h, vrow=vrow, buf=buf: h.dma_start(
                    out=VBUF[:, buf, :],
                    in_=d_kvgath[vrow:vrow + 128, :]),
                    waits=[], inc=(f"skv{buf}", 16), chk=f"kvld_{i}")

            def emit_S(i):
                qb, t, kn = steps[i]
                buf = i % 6
                kbv = KBUF[:, buf, :].rearrange("p (c t) -> p c t", c=8)
                for dc in range(8):
                    kb.do("pe", lambda h, dc=dc, kbv=kbv, qb=qb, i=i: h.matmul(
                        PS[i % 2][:, :], kbv[:, dc, :],
                        Q[:, dc, qb * 256:(qb + 1) * 256],
                        start=(dc == 0), stop=(dc == 7)),
                        waits=[f"kvld_{i}", f"kvldK_{i}", f"qevtb{qb // 2}",
                               (f"P_{i-2}" if i >= 2 else None),
                               ("qall" if i == 0 else None)],
                        inc=(("spe", 1) if dc == 7 else None),
                        chk=(f"S_{i}" if dc == 7 else None))
                j = t - 8 * qb
                if j >= 0:
                    kb.do("dve", lambda h, i=i, j=j: h.tensor_tensor(
                        PS[i % 2][:, :], PS[i % 2][:, :], MSKC[:, j, :], ALU.add),
                        waits=[f"S_{i}", "msk_ld"],
                        inc=("sdve", 1), chk=f"MS_{i}")
                kb.do("act", lambda h, i=i: h.activation(
                    PTS[:, i % 2, :], PS[i % 2][:, :], AF.Exp),
                    waits=[f"S_{i}", (f"MS_{i}" if j >= 0 else None),
                           (f"AV_{i-2}" if i >= 2 else None)],
                    inc=("sact", 1), chk=f"P_{i}")

            def emit_AV(i):
                qb, t, kn = steps[i]
                buf = i % 6
                first = (t == 0)
                lastt = (t == kn - 1)
                for qs in range(2):
                    for dn in range(2):
                        kb.do("pe", lambda h, i=i, qs=qs, dn=dn, buf=buf: h.matmul(
                            PO[qs][:, dn * 512:(dn + 1) * 512],
                            PTS[:, i % 2, qs * 128:(qs + 1) * 128],
                            VBUF[:, buf, dn * 512:(dn + 1) * 512],
                            start=first, stop=lastt, skip_group_check=True),
                            waits=[f"P_{i}",
                                   ((f"attev_{qb-1}" if qb >= 1 else "qall")
                                    if first and qs == 0 and dn == 0 else None)],
                            inc=None)
                    kb.do("pe", lambda h, i=i, qs=qs: h.matmul(
                        (PLA if qs == 0 else PLB)[:, 0:1],
                        PTS[:, i % 2, qs * 128:(qs + 1) * 128],
                        ONESC[:, :],
                        start=first, stop=lastt,
                        skip_group_check=True),
                        waits=[],
                        inc=(("spe", 1) if qs == 1 else None),
                        chk=(f"AV_{i}" if qs == 1 else None))

            def emit_evac(qb, i):
                kb.do("dve", lambda h: h.reciprocal(LINV[:, 0:1], PLA[:, 0:1]),
                      waits=[f"AV_{i}"], inc=("sdve", 1))
                kb.do("dve", lambda h: h.reciprocal(LINV[:, 1:2], PLB[:, 0:1]),
                      inc=("sdve", 1), chk=f"linv_{qb}")
                for qs in range(2):
                    kb.do("dve", lambda h, qb=qb, qs=qs: h.scalar_tensor_tensor(
                        X[:, 2 * qb + qs, :], PO[qs][:, :], LINV[:, qs:qs + 1],
                        X[:, 2 * qb + qs, :], ALU.mult, ALU.add),
                        waits=[f"linv_{qb}"],
                        inc=("sdve", 1),
                        chk=(f"attev_{qb}" if qs == 1 else None))

            for i0 in range(6):
                kv_load(i0)
            emit_S(0)
            for i in range(nsteps):
                if i + 1 < nsteps:
                    emit_S(i + 1)
                emit_AV(i)
                if i + 6 < nsteps:
                    kv_load(i + 6)
                qb, t, kn = steps[i]
                if t == kn - 1:
                    emit_evac(qb, i)
                    if on_evac is not None:
                        on_evac(qb)

    Q_box = [None]

    with nc.sbuf_tensor("MEM0", [8, TCORE], BF16) as MEM0, \
         nc.sbuf_tensor("MPP0", [128, 2, TCORE], BF16) as MPP0, \
         nc.sbuf_tensor("HS0", [128, 2, 2, 1024], BF16) as HS0, \
         nc.sbuf_tensor("WB0", [128, 16384], BF16) as WB0:
        wb_any = WB0[:, 0:1]
        wb_w1 = lambda j: _view(WB0, wb_any, j * 4096, [[512, 8], [1, 512]])
        wb_w2 = lambda j: _view(WB0, wb_any, 8192 + j * 4096, [[1024, 4], [1, 1024]])
        emit_wpair_load(0, 1, wb_w1, wb_w2, ())
        emit_prep(0, True, MEM0,
                  lambda tau: f"xg{tau // 4}",
                  None, None)
        def _ln2_stats(t):
            emit_stats_tile(1, 2 * t, waits=[f"f0_yev_3_{t}"])
            emit_stats_tile(1, 2 * t + 1)
        emit_ffn(0, 0, MEM0, MPP0, HS0, wb_w1, wb_w2, None, None, None,
                 on_yev=_ln2_stats)

    emit_prep(1, False, None, lambda tau: None,
              kb.chk["ffngelu0"], kb.chk["ffndone0"], stats_done=True)

    q_scope = contextlib.ExitStack()
    Q = q_scope.enter_context(nc.sbuf_tensor("Q", [128, 8, TCORE], BF16))
    Q_box[0] = Q
    with nc.sbuf_tensor("WVV", [128, 8, 1024], BF16) as WVV, \
         nc.sbuf_tensor("KST", [128, 8, 512], BF16) as KST, \
         nc.sbuf_tensor("VST", [128, 8, 512], BF16) as VST:
        _j = JUNK[:, :]
        BVR = bass.AP(_j.tensor, 0, [[_j.ap[0][0], 1], [1, D]])
        g_dma(wa_wq, d_wqT.rearrange("(ic p) o -> p ic o", p=128), None,
              waits=["f0_y_2_7"], sem="swa")
        g_dma(wa_wk, d_wkT.rearrange("(ic p) o -> p ic o", p=128), "wqk_ld",
              waits=[], sem="swa")
        g_dma(WVV[:, :, :], d_wvT.rearrange("(ic p) o -> p ic o", p=128),
              None, waits=["f0_y_3_7", "f0_hm_3_7_1"], sem="swb")
        g_dma(BVR, d_bv, "wv_ld", waits=["var1_15"], sem="swb")
        emit_qkv(WVV, KST, VST, BVR)

    with nc.sbuf_tensor("MSKC", [128, 8, 256], F32) as MSKC, \
         nc.sbuf_tensor("KBUF", [128, 6, 1024], BF16) as KBUF, \
         nc.sbuf_tensor("VBUF", [128, 6, 1024], BF16) as VBUF:
        kb.do("sp", lambda h: h.dma_start(
            out=MSKC[:, :, :], in_=d_msk.rearrange("j p q -> p j q")),
            waits=["qall", "qmmall", ("skc3", 32, "dma")],
            inc=("smsk", 16), chk="msk_ld")
        emit_wpair_load(1, 0, wa_w1, wa_w2, waits=["qmmall", "qall"])
        def _ln3_stats(qb):
            emit_stats_tile(2, 2 * qb, waits=[f"attev_{qb}"])
            emit_stats_tile(2, 2 * qb + 1)
        emit_attention(MSKC, KBUF, VBUF, on_evac=_ln3_stats)
    q_scope.close()

    with nc.sbuf_tensor("MEM1", [8, TCORE], BF16) as MEM1, \
         nc.sbuf_tensor("MPP1", [128, 2, TCORE], BF16) as MPP1, \
         nc.sbuf_tensor("HS1", [128, 2, 2, 1024], BF16) as HS1, \
         nc.sbuf_tensor("WB1", [128, 16384], BF16) as WB1:
        wb_any1 = WB1[:, 0:1]
        wb_w1b = lambda j: _view(WB1, wb_any1, j * 4096, [[512, 8], [1, 512]])
        wb_w2b = lambda j: _view(WB1, wb_any1, 8192 + j * 4096, [[1024, 4], [1, 1024]])
        emit_wpair_load(1, 1, wb_w1b, wb_w2b, waits=["AV_287", "MS_287", "P_287"])
        emit_prep(2, True, MEM1, lambda tau: None,
                  kb.chk["P_287"], kb.chk["attev_7"], stats_done=True)
        emit_ffn(1, 2, MEM1, MPP1, HS1, wb_w1b, wb_w2b,
                 kb.chk["AV_287"], kb.chk["attev_7"], kb.chk["P_287"])

    for t in range(NT):
        kb.do("sp", lambda h, t=t: h.dma_start(
            out=d_out.rearrange("(t p) d -> p t d", p=128)[:, t, :],
            in_=X[:, t, :]),
            waits=[f"f1_yev_3_{t // 2}"], inc=("sout", 16))
    kb.do("sp", lambda h: h.wait_ge(kb.sems["sout"], 16 * NT), waits=[])

    with nc.Block() as block:
        @block.tensor
        def _(h):
            for c in kb.stream["pe"]:
                c(h)

        @block.scalar
        def _(h):
            for c in kb.stream["act"]:
                c(h)

        @block.vector
        def _(h):
            for c in kb.stream["dve"]:
                c(h)

        @block.sync
        def _(h):
            for c in kb.stream["sp"]:
                c(h)

        @block.gpsimd
        def _(h):
            for c in kb.stream["pool"]:
                c(h)

    kb.es.close()
    return nc


_NC_CACHE = [None]


def kernel(L0_attn_X, L0_attn_Wq, L0_attn_bq, L0_attn_Wk, L0_attn_bk,
           L0_attn_Wv, L0_attn_bv, L0_ffn_gate_w, L0_ffn_w1, L0_ffn_b1,
           L0_ffn_w2, L0_ffn_b2):
    Xin = np.asarray(L0_attn_X)
    wqT = np.ascontiguousarray(np.asarray(L0_attn_Wq).T)
    wkT = np.ascontiguousarray(np.asarray(L0_attn_Wk).T)
    wvT = np.ascontiguousarray(np.asarray(L0_attn_Wv).T)
    bq = np.ascontiguousarray(np.asarray(L0_attn_bq).reshape(8, 128))
    bk = np.ascontiguousarray(np.asarray(L0_attn_bk).reshape(8, 128))
    bv = np.ascontiguousarray(np.asarray(L0_attn_bv).reshape(1, 1024))
    gw = np.ascontiguousarray(np.asarray(L0_ffn_gate_w))
    w1 = np.ascontiguousarray(np.asarray(L0_ffn_w1))
    b1 = np.ascontiguousarray(np.asarray(L0_ffn_b1))
    w2 = np.ascontiguousarray(np.asarray(L0_ffn_w2))
    b2 = np.ascontiguousarray(np.asarray(L0_ffn_b2))
    idf = np.eye(128, dtype=np.float32)
    eb = np.ascontiguousarray(
        (np.eye(8, dtype=np.float32)[:, :, None]
         * np.ones((1, 1, 128), np.float32)).reshape(8, 8 * 128))

    if _NC_CACHE[0] is None:
        _NC_CACHE[0] = build_program()
    nc = _NC_CACHE[0]

    in_maps = []
    for core in range(NCORE):
        s, c = core // 4, core % 4
        blocks = [Xin[s, 1024 * j + 256 * c:1024 * j + 256 * c + 256]
                  for j in range(8)]
        xs = np.ascontiguousarray(np.concatenate(blocks, axis=0))
        ki = np.arange(128)[:, None]
        mi = np.arange(256)[None, :]
        msk = np.stack([np.where(128 * j + ki > 256 * c + mi, -1e30, 0.0)
                        .astype(np.float32) for j in range(8)])
        in_maps.append({
            "x": xs, "wqT": wqT, "wkT": wkT, "wvT": wvT,
            "bq": bq, "bk": bk, "bv": bv, "gw": gw,
            "w1": w1, "b1": b1, "w2": w2, "b2": b2,
            "msk": np.ascontiguousarray(msk), "idf": idf, "eb": eb,
        })

    res = run_bass_kernel_spmd(nc, in_maps, core_ids=list(range(NCORE)))
    out = np.empty((2, 8192, 1024), dtype=np.float32)
    for core in range(NCORE):
        s, c = core // 4, core % 4
        o = res.results[core]["out"]
        for j in range(8):
            out[s, 1024 * j + 256 * c:1024 * j + 256 * c + 256] = \
                o[256 * j:256 * (j + 1)]
    return out


# version 19
# speedup vs baseline: 1.0703x; 1.0113x over previous
import contextlib
import numpy as np
import ml_dtypes
import concourse.bass as bass
import concourse.mybir as mybir
from concourse.bass_utils import run_bass_kernel_spmd

F32 = mybir.dt.float32
BF16 = mybir.dt.bfloat16
AF = mybir.ActivationFunctionType
ALU = mybir.AluOpType
AX = mybir.AxisListType

NCORE = 8
D = 1024
TCORE = 2048
NT = 16
NT2 = 8
NE = 8
HID = 512
NQ = 8
QSC = 1.0 / 32.0

ENGS = ("pe", "act", "dve", "sp", "pool")
DMA_SEMS = {"sx0", "sx1", "sx2", "sx3", "swc", "swa", "swb", "smsk",
             "skst0", "skst1", "skv0", "skv1", "skv2", "skv3", "skv4",
             "skv5", "svv0", "svv1", "svv2", "svv3", "svv4", "svv5", "sout", "scc0", "scc1", "scc2", "scc3",
             "skc0", "skc1", "skc2", "skc3"}


def _view(t_handle, any_ap, offset, pattern):
    return bass.AP(any_ap.tensor, offset, [[any_ap.ap[0][0], 128]] + pattern)


class KB:
    def __init__(self, nc):
        self.nc = nc
        self.stream = {e: [] for e in ENGS}
        self.cnt = {}
        self.chk = {}
        self.sems = {}
        self.last_wait = {e: {} for e in ENGS}
        self.es = contextlib.ExitStack()

    def sem(self, name):
        self.sems[name] = self.es.enter_context(self.nc.semaphore(name))
        self.cnt[name] = 0

    def do(self, eng, fn, waits=(), inc=None, chk=None):
        wl = []
        for w in waits:
            if w is None:
                continue
            if isinstance(w, str):
                w = self.chk[w]
            s, v, prod = w
            if self.last_wait[eng].get(s, -1) >= v:
                continue
            self.last_wait[eng][s] = v
            wl.append((s, v))
        if inc is not None:
            s, k = inc
            self.cnt[s] += k
            if chk is not None:
                prod = "dma" if s in DMA_SEMS else eng
                self.chk[chk] = (s, self.cnt[s], prod)
        sems = self.sems

        def closure(h, fn=fn, wl=wl, inc=inc):
            for s, v in wl:
                h.wait_ge(sems[s], v)
            r = fn(h)
            if inc is not None:
                r.then_inc(sems[inc[0]], inc[1])
        self.stream[eng].append(closure)

    def waitonly(self, eng, waits):
        self.do(eng, lambda h: None, waits=waits)


def build_program():
    nc = bass.Bass("TRN2", target_bir_lowering=False)
    kb = KB(nc)
    E = kb.es.enter_context

    d_x = nc.dram_tensor("x", [TCORE, D], F32, kind="ExternalInput").ap()
    d_wqT = nc.dram_tensor("wqT", [D, D], F32, kind="ExternalInput").ap()
    d_wkT = nc.dram_tensor("wkT", [D, D], F32, kind="ExternalInput").ap()
    d_wvT = nc.dram_tensor("wvT", [D, D], F32, kind="ExternalInput").ap()
    d_bq = nc.dram_tensor("bq", [8, 128], F32, kind="ExternalInput").ap()
    d_bk = nc.dram_tensor("bk", [8, 128], F32, kind="ExternalInput").ap()
    d_bv = nc.dram_tensor("bv", [1, D], F32, kind="ExternalInput").ap()
    d_gw = nc.dram_tensor("gw", [D, NE], F32, kind="ExternalInput").ap()
    d_w1 = nc.dram_tensor("w1", [NE, D, HID], F32, kind="ExternalInput").ap()
    d_b1 = nc.dram_tensor("b1", [NE, HID], F32, kind="ExternalInput").ap()
    d_w2 = nc.dram_tensor("w2", [NE, HID, D], F32, kind="ExternalInput").ap()
    d_b2 = nc.dram_tensor("b2", [NE, D], F32, kind="ExternalInput").ap()
    d_msk = nc.dram_tensor("msk", [8, 128, 256], F32, kind="ExternalInput").ap()
    d_idf = nc.dram_tensor("idf", [128, 128], F32, kind="ExternalInput").ap()
    d_eb = nc.dram_tensor("eb", [8, 8 * 128], F32, kind="ExternalInput").ap()
    d_out = nc.dram_tensor("out", [TCORE, D], F32, kind="ExternalOutput").ap()

    t_kvself = nc.dram_tensor("kvself", [2 * TCORE, D], BF16, kind="Internal")
    t_kvgath = nc.dram_tensor("kvgath", [8 * TCORE, D], BF16, kind="Internal")
    d_kvself = t_kvself.ap()
    d_kvgath = t_kvgath.ap()

    for s in ("sx0", "sx1", "sx2", "sx3", "swc", "swa", "swb", "smsk",
              "skst0", "skst1",
              "skv0", "skv1", "skv2", "skv3", "skv4", "skv5",
              "svv0", "svv1", "svv2", "svv3", "svv4", "svv5",
              "scc0", "scc1", "scc2", "scc3", "sout",
              "skc0", "skc1", "skc2", "skc3",
              "spe", "sdve", "sact", "spool"):
        kb.sem(s)

    X = E(nc.sbuf_tensor("X", [128, NT, D], F32))
    XHF = E(nc.sbuf_tensor("XHF", [128, 8, TCORE], BF16))
    XHT = E(nc.sbuf_tensor("XHT", [128, 2, D], BF16))
    WA = E(nc.sbuf_tensor("WA", [128, 16384], BF16))
    IDF = E(nc.sbuf_tensor("IDF", [128, 128], F32))
    IDB = E(nc.sbuf_tensor("IDB", [128, 128], BF16))
    GWS = E(nc.sbuf_tensor("GWS", [128, 8, 8], BF16))
    B1T = E(nc.sbuf_tensor("B1T", [128, 4, 8], F32))
    B2B = E(nc.sbuf_tensor("B2B", [8, D], BF16))
    BQT = E(nc.sbuf_tensor("BQT", [128, 8], F32))
    BKT = E(nc.sbuf_tensor("BKT", [128, 8], F32))
    ONESR = E(nc.sbuf_tensor("ONESR", [1, 128], BF16))
    EB = E(nc.sbuf_tensor("EB", [8, 8, 128], BF16))
    ONESC = E(nc.sbuf_tensor("ONESC", [128, 1], BF16))
    CONSTS = E(nc.sbuf_tensor("CONSTS", [128, 8], F32))
    EPSC = CONSTS[:, 0:1]
    CNID = CONSTS[:, 1:2]
    CONE = CONSTS[:, 2:3]
    CHALF = CONSTS[:, 3:4]
    CBIG = CONSTS[:, 4:5]
    CQS = CONSTS[:, 5:6]
    CZERO = CONSTS[:, 6:7]
    CPD = CONSTS[:, 7:8]
    STAT = E(nc.sbuf_tensor("STAT", [128, 7, NT], F32))
    GA = E(nc.sbuf_tensor("GA", [128, NT, 8], F32))
    MTM = E(nc.sbuf_tensor("MTM", [128, NT, 8], F32))
    GST = E(nc.sbuf_tensor("GST", [128, 6, NT], F32))
    GEXP3 = E(nc.sbuf_tensor("GEXP3", [128, NT, 8], F32))
    LINV = E(nc.sbuf_tensor("LINV", [128, 2], F32))
    PTS = E(nc.sbuf_tensor("PTS", [128, 2, 256], BF16))
    JUNK = E(nc.sbuf_tensor("JUNK", [128, D], BF16))

    wa_any = WA[:, 0:1]

    def wa_w1(j):
        return _view(WA, wa_any, j * 4096, [[512, 8], [1, 512]])

    def wa_w2(j):
        return _view(WA, wa_any, 8192 + j * 4096, [[1024, 4], [1, 1024]])

    wa_wq = _view(WA, wa_any, 0, [[1024, 8], [1, 1024]])
    wa_wk = _view(WA, wa_any, 8192, [[1024, 8], [1, 1024]])

    def g_dma(dst, src, chkname=None, waits=(), sem="swc"):
        kb.do("pool", lambda h, d=dst, s=src: h.dma_start(out=d, in_=s),
              waits=waits, inc=(sem, 16), chk=chkname)

    kb.do("pool", lambda h: h.memset(ONESR[:, :], 1.0), inc=("spool", 1))
    kb.do("pool", lambda h: h.memset(ONESC[:, :], 1.0), inc=("spool", 1))
    for col, val in ((0, 1e-5), (1, -1.0 / D), (2, 1.0), (3, 0.5),
                     (4, -1e30), (5, QSC), (6, 0.0), (7, 1.0 / D)):
        kb.do("pool", lambda h, c=col, v=val: h.memset(CONSTS[:, c:c + 1], v),
              inc=("spool", 1))
    kb.chk["memsets"] = ("spool", kb.cnt["spool"], "pool")

    g_dma(IDF[:, :], d_idf)
    g_dma(IDB[:, :], d_idf)
    g_dma(GWS[:, :, :], d_gw.rearrange("(ic p) e -> p ic e", p=128))
    g_dma(B2B[:, :], d_b2)
    g_dma(EB[:, :, :], d_eb.rearrange("k (e m) -> k e m", e=8))

    with nc.sbuf_tensor("B1SB", [8, HID], F32) as B1SB, \
         nc.sbuf_tensor("BQSB", [8, 128], F32) as BQSB, \
         nc.sbuf_tensor("BKSB", [8, 128], F32) as BKSB:
        g_dma(B1SB[:, :], d_b1)
        g_dma(BQSB[:, :], d_bq)
        g_dma(BKSB[:, :], d_bk, "consts_all")
        b1sb_ap = B1SB
        bqsb_ap = BQSB
        bksb_ap = BKSB

    for g in range(4):
        kb.do("sp", lambda h, g=g: h.dma_start(
            out=X[:, 4 * g:4 * g + 4, :],
            in_=d_x.rearrange("(t p) d -> p t d", p=128)[:, 4 * g:4 * g + 4, :]),
            inc=(f"sx{g}", 16), chk=f"xg{g}")

    def emit_wpair_load(fidx, p, slotview_w1, slotview_w2, waits):
        sem = "swa" if p % 2 == 0 else "swb"
        for j in range(2):
            g_dma(slotview_w1(j),
                  d_w1[2 * p + j].rearrange("(ic p) h -> p ic h", p=128),
                  f"wl_{fidx}_{p}_w1{j}", waits=waits, sem=sem)
        for j in range(2):
            g_dma(slotview_w2(j),
                  d_w2[2 * p + j].rearrange("(hc p) d -> p hc d", p=128),
                  f"wl_{fidx}_{p}_w2{j}", waits=waits, sem=sem)
        kb.chk[f"wload_{fidx}_{p}"] = kb.chk[f"wl_{fidx}_{p}_w21"]

    emit_wpair_load(0, 0, wa_w1, wa_w2, ())

    def emit_bvec_transposes(PBF):
        for hc in range(4):
            kb.do("pe", lambda h, hc=hc: h.transpose(
                PBF[:, hc * 8:hc * 8 + 8], b1sb_ap[0:8, hc * 128:(hc + 1) * 128],
                IDF[0:8, 0:8]),
                waits=["consts_all", "consts_all"],
                inc=("spe", 1), chk=f"b1tr{hc}")
        kb.do("pe", lambda h: h.transpose(
            PBF[:, 32:40], bqsb_ap[0:8, :], IDF[0:8, 0:8]),
            waits=["consts_all"], inc=("spe", 1), chk="bqtr")
        kb.do("pe", lambda h: h.transpose(
            PBF[:, 40:48], bksb_ap[0:8, :], IDF[0:8, 0:8]),
            waits=["consts_all"], inc=("spe", 1), chk="bktr")
        kb.do("dve", lambda h: h.tensor_copy(
            B1T[:, :, :], PBF[:, 0:32].rearrange("p (hc e) -> p hc e", hc=4)),
            waits=["b1tr3"], inc=("sdve", 1))
        kb.do("dve", lambda h: h.tensor_copy(BQT[:, :], PBF[:, 32:40]),
              waits=["bqtr"], inc=("sdve", 1))
        kb.do("dve", lambda h: h.tensor_copy(BKT[:, :], PBF[:, 40:48]),
              waits=["bktr"], inc=("sdve", 1), chk="bvecs")

    junk_chain = [None]

    def emit_stats_tile(stage, tau, waits=()):
        sums = STAT[:, 0, :]
        sumsq = STAT[:, 2, :]
        kb.do("dve", lambda h, t=tau: h.reduce_sum(
            sums[:, t:t + 1], X[:, t, :], AX.X),
            waits=list(waits) + ["memsets"],
            inc=("sdve", 1), chk=f"sum{stage}_{tau}")
        kb.do("dve", lambda h, t=tau: h.scalar_tensor_tensor(
            JUNK[:, :], X[:, t, :], CZERO, X[:, t, :],
            ALU.add, ALU.mult, accum_out=sumsq[:, t:t + 1]),
            waits=[junk_chain[0]],
            inc=("sdve", 1), chk=f"var{stage}_{tau}")
        junk_chain[0] = f"var{stage}_{tau}"

    def emit_prep(stage, gating, MEM, xwait, entry_pe, entry_dve,
                  stats_done=False):
        sums = STAT[:, 0, :]
        nmean = STAT[:, 1, :]
        sumsq = STAT[:, 2, :]
        m2 = STAT[:, 3, :]
        var_ = STAT[:, 4, :]
        std = STAT[:, 5, :]
        rstd = STAT[:, 6, :]

        with nc.psum_tensor("PT0_%d" % stage, [128, 512], BF16) as PT0, \
             nc.psum_tensor("PT1_%d" % stage, [128, 512], BF16) as PT1, \
             nc.psum_tensor("PG0_%d" % stage, [128, 8], F32) as PG0, \
             nc.psum_tensor("PG1_%d" % stage, [128, 8], F32) as PG1, \
             nc.psum_tensor("PBF_%d" % stage, [128, 512], F32) as PBF:
            PT = (PT0, PT1)
            PG = (PG0, PG1)
            if stage == 0:
                emit_bvec_transposes(PBF)

            if not stats_done:
                for tau in range(NT):
                    emit_stats_tile(stage, tau, waits=[xwait(tau)])
            gsz = 4 if stage == 0 else NT
            for g0 in range(0, NT, gsz):
                S4 = slice(g0, g0 + gsz)
                gn = g0 // gsz
                kb.do("dve", lambda h, S4=S4: h.tensor_scalar(
                    nmean[:, S4], sums[:, S4], CNID, None, ALU.mult),
                    waits=[f"sum{stage}_{g0+gsz-1}", f"var{stage}_{g0+gsz-1}"],
                    inc=("sdve", 1), chk=f"nmean{stage}g{gn}")
                kb.do("dve", lambda h, S4=S4: h.tensor_mul(
                    m2[:, S4], nmean[:, S4], nmean[:, S4]),
                    waits=[f"nmean{stage}g{gn}"], inc=("sdve", 1),
                    chk=f"m2{stage}g{gn}")
                kb.do("dve", lambda h, S4=S4: h.scalar_tensor_tensor(
                    var_[:, S4], sumsq[:, S4], CPD, m2[:, S4],
                    ALU.mult, ALU.subtract),
                    waits=[f"m2{stage}g{gn}"], inc=("sdve", 1),
                    chk=f"varb{stage}g{gn}")
                kb.do("act", lambda h, S4=S4: h.activation(
                    std[:, S4], var_[:, S4], AF.Sqrt, bias=EPSC, scale=1.0),
                    waits=[f"varb{stage}g{gn}", "memsets"],
                    inc=("sact", 1), chk=f"std{stage}g{gn}")
                kb.do("dve", lambda h, S4=S4: h.reciprocal(
                    rstd[:, S4], std[:, S4]),
                    waits=[f"std{stage}g{gn}"], inc=("sdve", 1),
                    chk=f"rstd{stage}g{gn}")

            for tau in range(NT):
                par = tau % 2
                kb.do("dve", lambda h, t=tau, par=par: h.tensor_scalar(
                    XHT[:, par, :], X[:, t, :], nmean[:, t:t + 1],
                    rstd[:, t:t + 1], ALU.add, ALU.mult),
                    waits=[f"rstd{stage}g{tau // gsz}",
                           f"tr{stage}_{tau-2}_1" if tau >= 2 else None],
                    inc=("sdve", 1), chk=f"app{stage}_{tau}")
                for grp in range(2):
                    for k in range(4):
                        dc = grp * 4 + k
                        kb.do("pe", lambda h, t=tau, par=par, dc=dc, grp=grp: h.transpose(
                            PT[(2 * t + grp) % 2][:, (dc % 4) * 128:(dc % 4 + 1) * 128],
                            XHT[:, par, dc * 128:(dc + 1) * 128], IDB[:, :]),
                            waits=[f"app{stage}_{tau}", "consts_all",
                                   (f"trev{stage}_{tau-1}_{grp}" if tau >= 1 else entry_dve),
                                   (entry_pe if tau == 0 and dc == 0 else None)],
                            inc=("spe", 1),
                            chk=(f"tr{stage}_{tau}_{grp}" if k == 3 else None))
                    kb.do("dve", lambda h, t=tau, grp=grp: h.tensor_copy(
                        XHF[:, 4 * grp:4 * grp + 4, t * 128:(t + 1) * 128],
                        PT[(2 * t + grp) % 2][:, 0:512].rearrange(
                            "p (a b) -> p a b", a=4)),
                        waits=[f"tr{stage}_{tau}_{grp}"],
                        inc=("sdve", 1), chk=f"trev{stage}_{tau}_{grp}")
                if gating:
                    for ic in range(8):
                        kb.do("pe", lambda h, t=tau, ic=ic, par=par: h.matmul(
                            PG[par][:, :], XHF[:, ic, t * 128:(t + 1) * 128],
                            GWS[:, ic, :], start=(ic == 0), stop=(ic == 7)),
                            waits=[f"trev{stage}_{tau}_1", "consts_all",
                                   f"gev{stage}_{tau-2}" if tau >= 2 else None],
                            inc=(("spe", 1) if ic == 7 else None),
                            chk=(f"g{stage}_{tau}" if ic == 7 else None))
                    kb.do("dve", lambda h, t=tau, par=par: h.tensor_copy(
                        GA[:, t, :], PG[par][:, :]),
                        waits=[f"g{stage}_{tau}"],
                        inc=("sdve", 1), chk=f"gev{stage}_{tau}")

            kb.chk[f"prepdone{stage}"] = kb.chk[f"trev{stage}_{NT-1}_1"]

            if gating:
                g_s1 = GST[:, 0, :]
                g_z = GST[:, 1, :]
                g_s2 = GST[:, 2, :]
                g_d = GST[:, 3, :]
                g_e2 = GST[:, 4, :]
                g_msc = GST[:, 5, :]

                ggs = 4 if stage == 0 else NT
                for gg0 in range(0, NT, ggs):
                    GS = slice(gg0, gg0 + ggs)
                    s1g = g_s1[:, GS]
                    zg = g_z[:, GS]
                    s2g = g_s2[:, GS]
                    dg = g_d[:, GS]
                    e2g = g_e2[:, GS]
                    mscg = g_msc[:, GS]
                    gag = GA[:, GS, :]
                    geg = GEXP3[:, GS, :]
                    mtg = MTM[:, GS, :]

                    def bc3(ap2d, n=ggs):
                        return bass.AP(ap2d.tensor, ap2d.offset,
                                       [[ap2d.ap[0][0], 128], [1, n], [0, 8]])

                    nstp = [0]

                    def gb(fn, waits=(), engine="dve", gg0=gg0, nstp=nstp):
                        nstp[0] += 1
                        name = f"gb{stage}_{gg0}_{nstp[0]}"
                        prev = (f"gb{stage}_{gg0}_{nstp[0]-1}"
                                if nstp[0] > 1 else None)
                        kb.do(engine, fn, waits=list(waits) + [prev],
                              inc=(("sdve", 1) if engine == "dve"
                                   else ("sact", 1)),
                              chk=name)
                        return name

                    gb(lambda h, a=s1g, b=gag: h.reduce_max(a, b, AX.X),
                       waits=[f"gev{stage}_{gg0+ggs-1}"])
                    gb(lambda h, a=geg, b=gag, c=bc3(s1g): h.tensor_tensor(
                        a, b, c, ALU.subtract))
                    gb(lambda h, a=mtg, b=geg: h.activation(a, b, AF.Exp),
                       engine="act")
                    gb(lambda h, a=zg, b=mtg: h.reduce_sum(a, b, AX.X))
                    gb(lambda h, a=geg, b=gag, c=bc3(s1g): h.tensor_tensor(
                        a, b, c, ALU.is_ge))
                    gb(lambda h, a=mtg, b=geg, c=gag: h.scalar_tensor_tensor(
                        a, b, CBIG, c, ALU.mult, ALU.add))
                    gb(lambda h, a=s2g, b=mtg: h.reduce_max(a, b, AX.X))
                    gb(lambda h, a=dg, b=s2g, c=s1g: h.tensor_sub(a, b, c))
                    gb(lambda h, a=e2g, b=dg: h.activation(a, b, AF.Exp),
                       engine="act")
                    gb(lambda h, a=dg, b=zg: h.reciprocal(a, b))
                    gb(lambda h, a=e2g: h.tensor_scalar(a, a, CONE, None,
                                                        ALU.add))
                    gb(lambda h, a=mscg, b=e2g, c=dg: h.tensor_mul(a, b, c))
                    gb(lambda h, a=mscg: h.tensor_scalar(a, a, CHALF, None,
                                                         ALU.mult))
                    gb(lambda h, a=geg, b=gag, c=bc3(s2g): h.tensor_tensor(
                        a, b, c, ALU.is_ge))
                    last = gb(lambda h, a=mtg, b=geg, c=bc3(mscg):
                              h.tensor_tensor(a, b, c, ALU.mult))
                    for tau in range(gg0, gg0 + ggs):
                        kb.chk[f"mtm_{stage}_{tau}"] = kb.chk[last]

                for g4 in range(4):
                    for k in range(4):
                        tau = g4 * 4 + k
                        kb.do("pe", lambda h, tau=tau, k=k: h.transpose(
                            PBF[0:8, k * 128:(k + 1) * 128], MTM[:, tau, :],
                            IDF[:, :]),
                            waits=[f"mtm_{stage}_{tau}",
                                   f"memev_{stage}_{g4-1}" if g4 >= 1 else ("bvecs" if stage == 0 else None)],
                            inc=("spe", 1),
                            chk=(f"mtr_{stage}_{g4}" if k == 3 else None))
                    kb.do("dve", lambda h, g4=g4: h.tensor_copy(
                        MEM[0:8, g4 * 512:(g4 + 1) * 512], PBF[0:8, 0:512]),
                        waits=[f"mtr_{stage}_{g4}"],
                        inc=("sdve", 1), chk=f"memev_{stage}_{g4}")
                kb.chk[f"memdone{stage}"] = kb.chk[f"memev_{stage}_3"]
                for g4 in range(4):
                    kb.chk[f"memdone{stage}_g{g4}"] = kb.chk[f"memev_{stage}_{g4}"]

    def emit_ffn(fidx, stage, MEM, MPP, HS, wb_w1, wb_w2, entry_pe, entry_dve,
                 entry_gelu, on_yev=None):
        with nc.psum_tensor("PH0_%d" % fidx, [128, 1024], F32) as PH0, \
             nc.psum_tensor("PH1_%d" % fidx, [128, 1024], F32) as PH1, \
             nc.psum_tensor("PY0_%d" % fidx, [128, 1024], F32) as PY0, \
             nc.psum_tensor("PY1_%d" % fidx, [128, 1024], F32) as PY1:
            PH = (PH0, PH1)
            PY = (PY0, PY1)
            for p in range(4):
                w1v = wa_w1 if p % 2 == 0 else wb_w1
                w2v = wa_w2 if p % 2 == 0 else wb_w2
                for j in range(2):
                    e = 2 * p + j
                    for qr in range(4):
                        hb = qr % 2
                        kb.do("pe", lambda h, e=e, qr=qr, hb=hb: h.matmul(
                            PH0[:, hb * 512:(hb + 1) * 512], EB[0:8, e, :],
                            MEM[0:8, qr * 512:(qr + 1) * 512],
                            start=True, stop=True),
                            waits=[f"memdone{stage}_g{qr}", "consts_all", kb.chk.get("memsets"),
                                   (f"f{fidx}_gelu_{p-1}_7_0" if p >= 1 else entry_gelu),
                                   (entry_pe if p == 0 and j == 0 and qr == 0 else None),
                                   (f"mbev_{fidx}_{p}_{j}_{qr-2}" if qr >= 2 else
                                    (f"mbev_{fidx}_{p}_{j-1}_{qr+2}" if j >= 1 else
                                     (f"mbev_{fidx}_{p-1}_1_{qr+2}" if p >= 1 else None)))],
                            inc=("spe", 1), chk=f"mb_{fidx}_{p}_{j}_{qr}")
                        kb.do("dve", lambda h, j=j, qr=qr, hb=hb: h.tensor_copy(
                            MPP[:, j, qr * 512:(qr + 1) * 512],
                            PH0[:, hb * 512:(hb + 1) * 512]),
                            waits=[f"mb_{fidx}_{p}_{j}_{qr}",
                                   (f"f{fidx}_hm_{p-1}_7_{j}" if p >= 1 else None)],
                            inc=("sdve", 1), chk=f"mbev_{fidx}_{p}_{j}_{qr}")

                def emit_h(t, j):
                    e = 2 * p + j
                    wchk = f"wl_{fidx}_{p}_w1{j}"
                    for hc in range(4):
                        for ic in range(8):
                            last = (hc == 3 and ic == 7)
                            kb.do("pe", lambda h, t=t, j=j, hc=hc, ic=ic, w1v=w1v:
                                  h.matmul(PH[j][:, hc * 256:(hc + 1) * 256],
                                           w1v(j)[:, ic, hc * 128:(hc + 1) * 128],
                                           XHF[:, ic, t * 256:(t + 1) * 256],
                                           start=(ic == 0),
                                           stop=(ic == 7)),
                                  waits=[wchk, f"trev{stage}_{2*t+1}_1",
                                         f"mbev_{fidx}_{p}_1_3" if j == 0 and t == 0 else None,
                                         (f"f{fidx}_gelu_{p}_{t-1}_{j}" if t >= 1 else
                                          (f"f{fidx}_gelu_{p-1}_7_{j}" if p >= 1 else None))],
                                  inc=(("spe", 1) if last else None),
                                  chk=(f"f{fidx}_h_{p}_{t}_{j}" if last else None))
                    tp = t % 2
                    if t >= 2:
                        hswar = f"f{fidx}_y_{p}_{t-2}"
                    elif p >= 1:
                        hswar = f"f{fidx}_y_{p-1}_{6 + tp}"
                    else:
                        hswar = entry_pe if fidx == 1 else None
                    for hc in range(4):
                        kb.do("act", lambda h, t=t, j=j, hc=hc, e=e, tp=tp: h.activation(
                            HS[:, j, tp, hc * 256:(hc + 1) * 256],
                            PH[j][:, hc * 256:(hc + 1) * 256],
                            AF.Gelu, bias=B1T[:, hc, e:e + 1]),
                            waits=[f"f{fidx}_h_{p}_{t}_{j}", "bvecs", hswar],
                            inc=("sact", 1),
                            chk=(f"f{fidx}_gelu_{p}_{t}_{j}" if hc == 3 else None))
                    mpslice = MPP[:, j, t * 256:(t + 1) * 256]
                    mp_b = bass.AP(mpslice.tensor, mpslice.offset,
                                   [[mpslice.ap[0][0], 128], [0, 4], [1, 256]])
                    hview = HS[:, j, tp, :].rearrange("p (a b) -> p a b", a=4)
                    kb.do("dve", lambda h, hview=hview, mp_b=mp_b: h.tensor_tensor(
                        hview, hview, mp_b, ALU.mult),
                        waits=[f"f{fidx}_gelu_{p}_{t}_{j}", f"mbev_{fidx}_{p}_{j}_3"],
                        inc=("sdve", 1), chk=f"f{fidx}_hm_{p}_{t}_{j}")

                def emit_y(t):
                    if p == 0:
                        for ts in range(2):
                            for dn in range(2):
                                kb.do("pe", lambda h, t=t, ts=ts, dn=dn: h.matmul(
                                    PY[ts][:, dn * 512:(dn + 1) * 512],
                                    MEM[0:8, (2 * t + ts) * 128:(2 * t + ts + 1) * 128],
                                    B2B[0:8, dn * 512:(dn + 1) * 512],
                                    start=True, stop=False, skip_group_check=True),
                                    waits=[f"memdone{stage}_g{(2*t+ts)//4}", "consts_all",
                                           (f"f{fidx}_yev_{p}_{t-1}" if t >= 1 else entry_dve)],
                                    inc=None)
                    for j in range(2):
                        wchk = f"wl_{fidx}_{p}_w2{j}"
                        for ts in range(2):
                            for hc in range(4):
                                for dn in range(2):
                                    first = (p > 0 and j == 0 and hc == 0)
                                    last = (j == 1 and ts == 1 and hc == 3 and dn == 1)
                                    kb.do("pe", lambda h, t=t, j=j, ts=ts, hc=hc, dn=dn, w2v=w2v, st=first:
                                          h.matmul(PY[ts][:, dn * 512:(dn + 1) * 512],
                                                   HS[:, j, t % 2, hc * 256 + ts * 128:hc * 256 + (ts + 1) * 128],
                                                   w2v(j)[:, hc, dn * 512:(dn + 1) * 512],
                                                   start=st,
                                                   stop=(j == 1 and hc == 3),
                                                   skip_group_check=True),
                                          waits=[wchk, f"f{fidx}_hm_{p}_{t}_{j}",
                                                 (f"f{fidx}_yev_{p}_{t-1}" if (p > 0 and t >= 1) else None),
                                                 (f"f{fidx}_yev_{p-1}_7" if (p > 0 and t == 0) else None)],
                                          inc=(("spe", 1) if last else None),
                                          chk=(f"f{fidx}_y_{p}_{t}" if last else None))
                    for ts in range(2):
                        kb.do("dve", lambda h, t=t, ts=ts: h.tensor_tensor(
                            X[:, 2 * t + ts, :], PY[ts][:, :], X[:, 2 * t + ts, :],
                            ALU.add),
                            waits=[f"f{fidx}_y_{p}_{t}",
                                   (f"f{fidx}_yev_{p-1}_{t}" if p >= 1 else None)],
                            inc=("sdve", 1),
                            chk=(f"f{fidx}_yev_{p}_{t}" if ts == 1 else None))
                    if p == 3 and on_yev is not None:
                        on_yev(t)

                emit_h(0, 0)
                emit_h(0, 1)
                for t in range(1, NT2):
                    emit_h(t, 0)
                    emit_h(t, 1)
                    emit_y(t - 1)
                emit_y(NT2 - 1)

                if p + 2 <= 3:
                    tgt_w1 = wa_w1 if (p + 2) % 2 == 0 else wb_w1
                    tgt_w2 = wa_w2 if (p + 2) % 2 == 0 else wb_w2
                    emit_wpair_load(fidx, p + 2,
                                    lambda j, v=tgt_w1: v(j),
                                    lambda j, v=tgt_w2: v(j),
                                    waits=[f"f{fidx}_y_{p}_{NT2-1}"])
            kb.chk[f"ffndone{fidx}"] = kb.chk[f"f{fidx}_yev_3_{NT2-1}"]
            kb.chk[f"ffngelu{fidx}"] = kb.chk[f"f{fidx}_gelu_3_{NT2-1}_1"]

    def emit_qkv(WVV, KST, VST, BVR):
        with nc.psum_tensor("PQ0", [128, 512], F32) as PQ0, \
             nc.psum_tensor("PQ1", [128, 512], F32) as PQ1:
            PQ = (PQ0, PQ1)
            pq = 0
            pq_chk = {}

            def pq_wait():
                return pq_chk.get(pq % 2)

            for tb in range(4):
                for dc in range(8):
                    par = pq % 2
                    for ic in range(8):
                        kb.do("pe", lambda h, dc=dc, tb=tb, ic=ic, par=par: h.matmul(
                            PQ[par][:, :], wa_wk[:, ic, dc * 128:(dc + 1) * 128],
                            XHF[:, ic, tb * 512:(tb + 1) * 512],
                            start=(ic == 0), stop=(ic == 7)),
                            waits=["wqk_ld", "prepdone1", "ffngelu0", "ffndone0",
                                   pq_wait()],
                            inc=(("spe", 1) if ic == 7 else None),
                            chk=(f"kmm_{tb}_{dc}" if ic == 7 else None))
                    kb.do("dve", lambda h, dc=dc, par=par: h.tensor_scalar(
                        KST[:, dc, :], PQ[par][:, :], BKT[:, dc:dc + 1], None,
                        ALU.add),
                        waits=[f"kmm_{tb}_{dc}", "bvecs",
                               ((f"skc{tb-1}", 32, "dma") if (tb >= 1 and dc == 0)
                                else None)],
                        inc=("sdve", 1), chk=f"kev_{tb}_{dc}")
                    pq_chk[par] = f"kev_{tb}_{dc}"
                    pq += 1
                dst = d_kvself[tb * 1024:tb * 1024 + 512, :].rearrange(
                    "(b p) (c t) -> p c b t", p=128, c=8)
                kb.do("sp", lambda h, dst=dst, tb=tb: h.dma_start(
                    out=dst,
                    in_=KST[:, :, :].rearrange("p c (b t) -> p c b t", b=4)),
                    waits=[f"kev_{tb}_7"], inc=(f"skc{tb}", 16),
                    chk=f"kst_{tb}")
                for k4 in range(8):
                    ts16 = 4 * tb + k4 // 2
                    dn = k4 % 2
                    par = pq % 2
                    for ic in range(8):
                        kb.do("pe", lambda h, ts16=ts16, dn=dn, ic=ic, par=par:
                              h.matmul(PQ[par][:, :],
                                       XHF[:, ic, ts16 * 128:(ts16 + 1) * 128],
                                       WVV[:, ic, dn * 512:(dn + 1) * 512],
                                       start=(ic == 0), stop=False),
                              waits=["wv_ld", pq_wait()],
                              inc=None)
                    kb.do("pe", lambda h, dn=dn, par=par, BVR=BVR: h.matmul(
                        PQ[par][:, :], ONESR[:, :], BVR[0:1, dn * 512:(dn + 1) * 512],
                        start=False, stop=True),
                        waits=["memsets"],
                        inc=("spe", 1), chk=f"vmm_{tb}_{k4}")
                    kb.do("dve", lambda h, par=par, k4=k4: h.tensor_copy(
                        VST[:, k4, :], PQ[par][:, :]),
                        waits=[f"vmm_{tb}_{k4}",
                               ((f"skc{tb-1}", 32, "dma") if (tb >= 1 and k4 == 0)
                                else None)],
                        inc=("sdve", 1), chk=f"vev_{tb}_{k4}")
                    pq_chk[par] = f"vev_{tb}_{k4}"
                    pq += 1
                dst = d_kvself[tb * 1024 + 512:(tb + 1) * 1024, :].rearrange(
                    "(b p) (n d) -> p b n d", p=128, n=2)
                kb.do("sp", lambda h, dst=dst, tb=tb: h.dma_start(
                    out=dst,
                    in_=VST[:, :, :].rearrange("p (b n) d -> p b n d", n=2)),
                    waits=[f"vev_{tb}_7"], inc=(f"skc{tb}", 16),
                    chk=f"vst_{tb}")
                kb.do("pool", lambda h, tb=tb: h.collective_compute(
                    "AllGather", ALU.bypass,
                    replica_groups=[[0, 1, 2, 3], [4, 5, 6, 7]],
                    ins=[d_kvself[tb * 1024:(tb + 1) * 1024, :].bitcast(F32).opt()],
                    outs=[d_kvgath[tb * 4096:(tb + 1) * 4096, :].bitcast(F32).opt()]),
                    waits=[(f"skc{tb}", 32, "dma")],
                    inc=(f"scc{tb}", 1), chk=f"cc{tb}")

            for tb in range(4):
                for dc in range(8):
                    par = pq % 2
                    for ic in range(8):
                        kb.do("pe", lambda h, dc=dc, tb=tb, ic=ic, par=par: h.matmul(
                            PQ[par][:, :], wa_wq[:, ic, dc * 128:(dc + 1) * 128],
                            XHF[:, ic, tb * 512:(tb + 1) * 512],
                            start=(ic == 0), stop=(ic == 7)),
                            waits=["wqk_ld", pq_wait()],
                            inc=(("spe", 1) if ic == 7 else None),
                            chk=(f"qmm_{tb}_{dc}" if ic == 7 else None))
                    kb.do("dve", lambda h, dc=dc, tb=tb, par=par: h.tensor_scalar(
                        Q[:, dc, tb * 512:(tb + 1) * 512], PQ[par][:, :],
                        BQT[:, dc:dc + 1], CQS, ALU.add, ALU.mult),
                        waits=[f"qmm_{tb}_{dc}", "bvecs", "memsets"],
                        inc=("sdve", 1),
                        chk=f"qev_{tb}_{dc}")
                    pq_chk[par] = f"qev_{tb}_{dc}"
                    pq += 1
                kb.chk[f"qevtb{tb}"] = kb.chk[f"qev_{tb}_7"]
            kb.chk["qmmall"] = kb.chk["qmm_3_7"]
            kb.chk["qall"] = kb.chk["qev_3_7"]

    def emit_attention(MSKC, KBUF, VBUF, on_evac=None):
        with nc.psum_tensor("PS0", [128, 256], F32) as PS0, \
             nc.psum_tensor("PS1", [128, 256], F32) as PS1, \
             nc.psum_tensor("PO0", [128, 1024], F32) as PO0, \
             nc.psum_tensor("PO1", [128, 1024], F32) as PO1, \
             nc.psum_tensor("PLA", [128, 1], F32) as PLA, \
             nc.psum_tensor("PLB", [128, 1], F32) as PLB:
            PS = (PS0, PS1)
            PO = (PO0, PO1)
            steps = []
            for qb in range(NQ):
                kn = 8 * qb + 8
                for t in range(kn):
                    steps.append((qb, t, kn))
            nsteps = len(steps)

            def kv_load(i):
                qb, t, kn = steps[i]
                g = t
                v_ = (g // 2) % 4
                j2 = g // 8
                u_ = 2 * j2 + (g % 2)
                tbc = j2 // 2
                bp = u_ - 4 * tbc
                krow = 4096 * tbc + 1024 * v_ + 128 * bp
                vrow = krow + 512
                buf = i % 6
                w = [f"cc{tbc}",
                     (f"AV_{i-6}" if i >= 6 else "qmmall"),
                     ("qall" if i < 6 else None),
                     (("skc3", 32, "dma") if i < 6 else None)]
                kb.do("sp", lambda h, krow=krow, buf=buf: h.dma_start(
                    out=KBUF[:, buf, :],
                    in_=d_kvgath[krow:krow + 128, :]),
                    waits=w, inc=(f"skv{buf}", 16), chk=f"kvldK_{i}")
                kb.do("sp", lambda h, vrow=vrow, buf=buf: h.dma_start(
                    out=VBUF[:, buf, :],
                    in_=d_kvgath[vrow:vrow + 128, :]),
                    waits=[], inc=(f"skv{buf}", 16), chk=f"kvld_{i}")

            def emit_S(i):
                qb, t, kn = steps[i]
                buf = i % 6
                kbv = KBUF[:, buf, :].rearrange("p (c t) -> p c t", c=8)
                for dc in range(8):
                    kb.do("pe", lambda h, dc=dc, kbv=kbv, qb=qb, i=i: h.matmul(
                        PS[i % 2][:, :], kbv[:, dc, :],
                        Q[:, dc, qb * 256:(qb + 1) * 256],
                        start=(dc == 0), stop=(dc == 7)),
                        waits=[f"kvld_{i}", f"kvldK_{i}", f"qevtb{qb // 2}",
                               (f"P_{i-2}" if i >= 2 else None),
                               ("qall" if i == 0 else None)],
                        inc=(("spe", 1) if dc == 7 else None),
                        chk=(f"S_{i}" if dc == 7 else None))
                j = t - 8 * qb
                if j >= 0:
                    kb.do("dve", lambda h, i=i, j=j: h.tensor_tensor(
                        PS[i % 2][:, :], PS[i % 2][:, :], MSKC[:, j, :], ALU.add),
                        waits=[f"S_{i}", "msk_ld"],
                        inc=("sdve", 1), chk=f"MS_{i}")
                kb.do("act", lambda h, i=i: h.activation(
                    PTS[:, i % 2, :], PS[i % 2][:, :], AF.Exp),
                    waits=[f"S_{i}", (f"MS_{i}" if j >= 0 else None),
                           (f"AV_{i-2}" if i >= 2 else None)],
                    inc=("sact", 1), chk=f"P_{i}")

            def emit_AV(i):
                qb, t, kn = steps[i]
                buf = i % 6
                first = (t == 0)
                lastt = (t == kn - 1)
                for qs in range(2):
                    for dn in range(2):
                        kb.do("pe", lambda h, i=i, qs=qs, dn=dn, buf=buf: h.matmul(
                            PO[qs][:, dn * 512:(dn + 1) * 512],
                            PTS[:, i % 2, qs * 128:(qs + 1) * 128],
                            VBUF[:, buf, dn * 512:(dn + 1) * 512],
                            start=first, stop=lastt, skip_group_check=True),
                            waits=[f"P_{i}",
                                   ((f"attev_{qb-1}" if qb >= 1 else "qall")
                                    if first and qs == 0 and dn == 0 else None)],
                            inc=None)
                    kb.do("pe", lambda h, i=i, qs=qs: h.matmul(
                        (PLA if qs == 0 else PLB)[:, 0:1],
                        PTS[:, i % 2, qs * 128:(qs + 1) * 128],
                        ONESC[:, :],
                        start=first, stop=lastt,
                        skip_group_check=True),
                        waits=[],
                        inc=(("spe", 1) if qs == 1 else None),
                        chk=(f"AV_{i}" if qs == 1 else None))

            def emit_evac(qb, i):
                kb.do("dve", lambda h: h.reciprocal(LINV[:, 0:1], PLA[:, 0:1]),
                      waits=[f"AV_{i}"], inc=("sdve", 1))
                kb.do("dve", lambda h: h.reciprocal(LINV[:, 1:2], PLB[:, 0:1]),
                      inc=("sdve", 1), chk=f"linv_{qb}")
                for qs in range(2):
                    kb.do("dve", lambda h, qb=qb, qs=qs: h.scalar_tensor_tensor(
                        X[:, 2 * qb + qs, :], PO[qs][:, :], LINV[:, qs:qs + 1],
                        X[:, 2 * qb + qs, :], ALU.mult, ALU.add),
                        waits=[f"linv_{qb}"],
                        inc=("sdve", 1),
                        chk=(f"attev_{qb}" if qs == 1 else None))

            for i0 in range(6):
                kv_load(i0)
            emit_S(0)
            for i in range(nsteps):
                if i + 1 < nsteps:
                    emit_S(i + 1)
                emit_AV(i)
                if i + 6 < nsteps:
                    kv_load(i + 6)
                qb, t, kn = steps[i]
                if t == kn - 1:
                    emit_evac(qb, i)
                    if on_evac is not None:
                        on_evac(qb)

    Q_box = [None]

    with nc.sbuf_tensor("MEM0", [8, TCORE], BF16) as MEM0, \
         nc.sbuf_tensor("MPP0", [128, 2, TCORE], BF16) as MPP0, \
         nc.sbuf_tensor("HS0", [128, 2, 2, 1024], BF16) as HS0, \
         nc.sbuf_tensor("WB0", [128, 16384], BF16) as WB0:
        wb_any = WB0[:, 0:1]
        wb_w1 = lambda j: _view(WB0, wb_any, j * 4096, [[512, 8], [1, 512]])
        wb_w2 = lambda j: _view(WB0, wb_any, 8192 + j * 4096, [[1024, 4], [1, 1024]])
        emit_wpair_load(0, 1, wb_w1, wb_w2, ())
        emit_prep(0, True, MEM0,
                  lambda tau: f"xg{tau // 4}",
                  None, None)
        def _ln2_stats(t):
            emit_stats_tile(1, 2 * t, waits=[f"f0_yev_3_{t}"])
            emit_stats_tile(1, 2 * t + 1)
        emit_ffn(0, 0, MEM0, MPP0, HS0, wb_w1, wb_w2, None, None, None,
                 on_yev=_ln2_stats)

    emit_prep(1, False, None, lambda tau: None,
              kb.chk["ffngelu0"], kb.chk["ffndone0"], stats_done=True)

    q_scope = contextlib.ExitStack()
    Q = q_scope.enter_context(nc.sbuf_tensor("Q", [128, 8, TCORE], BF16))
    Q_box[0] = Q
    with nc.sbuf_tensor("WVV", [128, 8, 1024], BF16) as WVV, \
         nc.sbuf_tensor("KST", [128, 8, 512], BF16) as KST, \
         nc.sbuf_tensor("VST", [128, 8, 512], BF16) as VST:
        _j = JUNK[:, :]
        BVR = bass.AP(_j.tensor, 0, [[_j.ap[0][0], 1], [1, D]])
        g_dma(wa_wq, d_wqT.rearrange("(ic p) o -> p ic o", p=128), None,
              waits=["f0_y_2_7"], sem="swa")
        g_dma(wa_wk, d_wkT.rearrange("(ic p) o -> p ic o", p=128), "wqk_ld",
              waits=[], sem="swa")
        g_dma(WVV[:, :, :], d_wvT.rearrange("(ic p) o -> p ic o", p=128),
              None, waits=["f0_y_3_7", "f0_hm_3_7_1"], sem="swb")
        g_dma(BVR, d_bv, "wv_ld", waits=["var1_15"], sem="swb")
        emit_qkv(WVV, KST, VST, BVR)

    with nc.sbuf_tensor("MSKC", [128, 8, 256], F32) as MSKC, \
         nc.sbuf_tensor("KBUF", [128, 6, 1024], BF16) as KBUF, \
         nc.sbuf_tensor("VBUF", [128, 6, 1024], BF16) as VBUF:
        kb.do("sp", lambda h: h.dma_start(
            out=MSKC[:, :, :], in_=d_msk.rearrange("j p q -> p j q")),
            waits=["qall", "qmmall", ("skc3", 32, "dma")],
            inc=("smsk", 16), chk="msk_ld")
        emit_wpair_load(1, 0, wa_w1, wa_w2, waits=["qmmall", "qall"])
        def _ln3_stats(qb):
            emit_stats_tile(2, 2 * qb, waits=[f"attev_{qb}"])
            emit_stats_tile(2, 2 * qb + 1)
        emit_attention(MSKC, KBUF, VBUF, on_evac=_ln3_stats)
    q_scope.close()

    with nc.sbuf_tensor("MEM1", [8, TCORE], BF16) as MEM1, \
         nc.sbuf_tensor("MPP1", [128, 2, TCORE], BF16) as MPP1, \
         nc.sbuf_tensor("HS1", [128, 2, 2, 1024], BF16) as HS1, \
         nc.sbuf_tensor("WB1", [128, 16384], BF16) as WB1:
        wb_any1 = WB1[:, 0:1]
        wb_w1b = lambda j: _view(WB1, wb_any1, j * 4096, [[512, 8], [1, 512]])
        wb_w2b = lambda j: _view(WB1, wb_any1, 8192 + j * 4096, [[1024, 4], [1, 1024]])
        emit_wpair_load(1, 1, wb_w1b, wb_w2b, waits=["AV_287", "MS_287", "P_287"])
        emit_prep(2, True, MEM1, lambda tau: None,
                  kb.chk["P_287"], kb.chk["attev_7"], stats_done=True)
        emit_ffn(1, 2, MEM1, MPP1, HS1, wb_w1b, wb_w2b,
                 kb.chk["AV_287"], kb.chk["attev_7"], kb.chk["P_287"])

    for t in range(NT):
        kb.do("sp", lambda h, t=t: h.dma_start(
            out=d_out.rearrange("(t p) d -> p t d", p=128)[:, t, :],
            in_=X[:, t, :]),
            waits=[f"f1_yev_3_{t // 2}"], inc=("sout", 16))
    kb.do("sp", lambda h: h.wait_ge(kb.sems["sout"], 16 * NT), waits=[])

    with nc.Block() as block:
        @block.tensor
        def _(h):
            for c in kb.stream["pe"]:
                c(h)

        @block.scalar
        def _(h):
            for c in kb.stream["act"]:
                c(h)

        @block.vector
        def _(h):
            for c in kb.stream["dve"]:
                c(h)

        @block.sync
        def _(h):
            for c in kb.stream["sp"]:
                c(h)

        @block.gpsimd
        def _(h):
            for c in kb.stream["pool"]:
                c(h)

    kb.es.close()
    return nc


_NC_CACHE = [None]


def kernel(L0_attn_X, L0_attn_Wq, L0_attn_bq, L0_attn_Wk, L0_attn_bk,
           L0_attn_Wv, L0_attn_bv, L0_ffn_gate_w, L0_ffn_w1, L0_ffn_b1,
           L0_ffn_w2, L0_ffn_b2):
    Xin = np.asarray(L0_attn_X)
    wqT = np.ascontiguousarray(np.asarray(L0_attn_Wq).T)
    wkT = np.ascontiguousarray(np.asarray(L0_attn_Wk).T)
    wvT = np.ascontiguousarray(np.asarray(L0_attn_Wv).T)
    bq = np.ascontiguousarray(np.asarray(L0_attn_bq).reshape(8, 128))
    bk = np.ascontiguousarray(np.asarray(L0_attn_bk).reshape(8, 128))
    bv = np.ascontiguousarray(np.asarray(L0_attn_bv).reshape(1, 1024))
    gw = np.ascontiguousarray(np.asarray(L0_ffn_gate_w))
    w1 = np.ascontiguousarray(np.asarray(L0_ffn_w1))
    b1 = np.ascontiguousarray(np.asarray(L0_ffn_b1))
    w2 = np.ascontiguousarray(np.asarray(L0_ffn_w2))
    b2 = np.ascontiguousarray(np.asarray(L0_ffn_b2))
    idf = np.eye(128, dtype=np.float32)
    eb = np.ascontiguousarray(
        (np.eye(8, dtype=np.float32)[:, :, None]
         * np.ones((1, 1, 128), np.float32)).reshape(8, 8 * 128))

    if _NC_CACHE[0] is None:
        _NC_CACHE[0] = build_program()
    nc = _NC_CACHE[0]

    in_maps = []
    for core in range(NCORE):
        s, c = core // 4, core % 4
        blocks = [Xin[s, 1024 * j + 256 * c:1024 * j + 256 * c + 256]
                  for j in range(8)]
        xs = np.ascontiguousarray(np.concatenate(blocks, axis=0))
        ki = np.arange(128)[:, None]
        mi = np.arange(256)[None, :]
        msk = np.stack([np.where(128 * j + ki > 256 * c + mi, -1e30, 0.0)
                        .astype(np.float32) for j in range(8)])
        in_maps.append({
            "x": xs, "wqT": wqT, "wkT": wkT, "wvT": wvT,
            "bq": bq, "bk": bk, "bv": bv, "gw": gw,
            "w1": w1, "b1": b1, "w2": w2, "b2": b2,
            "msk": np.ascontiguousarray(msk), "idf": idf, "eb": eb,
        })

    res = run_bass_kernel_spmd(nc, in_maps, core_ids=list(range(NCORE)))
    out = np.empty((2, 8192, 1024), dtype=np.float32)
    for core in range(NCORE):
        s, c = core // 4, core % 4
        o = res.results[core]["out"]
        for j in range(8):
            out[s, 1024 * j + 256 * c:1024 * j + 256 * c + 256] = \
                o[256 * j:256 * (j + 1)]
    return out
